# version 01
# baseline (speedup 1.0000x reference)
import numpy as np
from contextlib import ExitStack

import concourse.bass as bass
import concourse.mybir as mybir
from concourse.bass_utils import run_bass_kernel_spmd
import ml_dtypes

F32 = mybir.dt.float32
BF16 = mybir.dt.bfloat16
ALU = mybir.AluOpType
AF = mybir.ActivationFunctionType
AX = mybir.AxisListType

H = 1024
II = 4096
NF = H // 128
NI = II // 128
ENGINES = ("sync", "vector", "scalar", "tensor", "gpsimd")


class Ctx:
    def __init__(self, nc, es):
        self.nc = nc
        self.es = es
        self.q = {e: [] for e in ENGINES}
        self.cnt = {}
        self.sems = {}
        self.trace = {e: [] for e in ENGINES}

    def sem(self, name):
        h = self.es.enter_context(self.nc.semaphore(name))
        self.sems[name] = h
        self.cnt[name] = 0

    def _line(self):
        import sys
        f = sys._getframe(2)
        return f.f_lineno

    def wait(self, eng, sem, val=None):
        v = self.cnt[sem] if val is None else val
        if v <= 0:
            return
        h = self.sems[sem]
        self.trace[eng].append(("w", sem, v, self._line()))
        self.q[eng].append(lambda E, h=h, v=v: E.wait_ge(h, v))

    def op(self, eng, build, inc=None, by=1):
        if inc is None:
            self.trace[eng].append(("o", None, 0, self._line()))
            self.q[eng].append(build)
            return None
        self.cnt[inc] += by
        h = self.sems[inc]
        self.trace[eng].append(("o", inc, self.cnt[inc], self._line()))
        self.q[eng].append(lambda E, b=build, h=h, by=by: b(E).then_inc(h, by))
        return self.cnt[inc]

    def c(self, sem):
        return self.cnt[sem]


def build_program(T, O, dbg=False, upto=None, noreuse=False):
    NT = T // 128
    NO = O // 128
    NS = NT + NO

    nc = bass.Bass("TRN2", target_bir_lowering=False)

    def din(name, shape, dt=F32):
        return nc.dram_tensor(name, shape, dt, kind="ExternalInput").ap()

    def dout(name, shape, dt=F32):
        return nc.dram_tensor(name, shape, dt, kind="ExternalOutput").ap()

    x_own = din("x_own", [T, H])
    x_oth = din("x_oth", [O, H])
    wdp_h = din("wdp", [H, H])
    wc_h = din("wc", [H, H])
    wres_h = din("wres", [H, H])
    wdl_h = din("wdl", [H, H])
    bd_h = din("bd", [128, NF])
    a1_h = din("a1", [128, NF])
    b1_h = din("b1", [128, NF])
    dvec_h = din("dvec", [128, 1])
    sin_h = din("sin", [128, 1])
    sind_h = din("sind", [128, 1])
    gw_h = din("gw", [II, H])
    uw_h = din("uw", [II, H])
    dw_h = din("dw", [H, II])
    ident_h = din("ident", [128, 128])
    ones1_h = din("ones1", [1, 128])
    gate_h = din("gate", [128, 1])

    out_h = dout("out", [T, H])
    dbg_h = {}
    if dbg:
        dbg_h["d_xn"] = dout("d_xn", [T, H])
        dbg_h["d_stats"] = dout("d_stats", [128, 2 * NS])
        dbg_h["d_ucs"] = dout("d_ucs", [128, 2 * NF])
        dbg_h["d_yat"] = dout("d_yat", [T, H])
        dbg_h["d_xq"] = dout("d_xq", [T, H])
        dbg_h["d_hq"] = dout("d_hq", [T, II])
        dbg_h["d_gq"] = dout("d_gq", [II, H])
        dbg_h["d_att"] = dout("d_att", [128, 4 * NF])
        dbg_h["d_parts"] = dout("d_parts", [128, 2 * NF * NS])
        dbg_h["d_cs0"] = dout("d_cs0", [H, 128])
        dbg_h["d_xnto"] = dout("d_xnto", [H, 128])
        dbg_h["d_wct"] = dout("d_wct", [H, 128])

    s_gq = nc.dram_tensor("s_gq", [II, H], BF16).ap()
    s_uq = nc.dram_tensor("s_uq", [II, H], BF16).ap()
    s_dq = nc.dram_tensor("s_dq", [H, II], BF16).ap()
    s_hq = nc.dram_tensor("s_hq", [T, II], BF16).ap()

    es = ExitStack()
    C = Ctx(nc, es)
    with es:
        def sb(name, shape, dt=F32, st=None):
            return (st or es).enter_context(nc.sbuf_tensor(name, shape, dt))

        acc = sb("acc", [128, NT * H])
        xqT = sb("xqT", [128, NF * T], BF16)
        m_all = sb("m_all", [128, NS])
        r_all = sb("r_all", [128, NS])
        uc = sb("uc", [128, NF])
        csum = sb("csum", [128, NF])
        b1uc = sb("b1uc", [128, NF])
        s_oth = sb("s_oth", [128, NF])
        carry = sb("carry", [128, NF])
        bdv = sb("bdv", [128, NF])
        a1v = sb("a1v", [128, NF])
        b1v = sb("b1v", [128, NF])
        dv = sb("dv", [128, 1])
        sinv = sb("sinv", [128, 1])
        sindv = sb("sindv", [128, 1])
        gatev = sb("gatev", [128, 1])
        identv = sb("identv", [128, 128])
        ones1v = sb("ones1v", [1, 128])
        rgs_x = sb("rgs_x", [128, 1])
        rgs_hd = sb("rgs_hd", [128, 1])
        scale_g = sb("scale_g", [128, 1])
        scale_u = sb("scale_u", [128, 1])
        scale_d = sb("scale_d", [128, 1])
        gsw = sb("gsw", [128, 128])
        rgsw = sb("rgsw", [128, 128])
        gmx_run = sb("gmx_run", [128, 8])
        gmx_parts = sb("gmx_parts", [128, 3 * NI])
        gmx1 = sb("gmx1", [4, 1])
        gmx_row = sb("gmx_row", [1, 128])
        q_amax = sb("q_amax", [128, 128])
        q_sc = sb("q_sc", [128, 128])
        q_t0 = sb("q_t0", [128, 128])
        q_step = sb("q_step", [128, 128])
        q_mi = sb("q_mi", [128, 128])
        q_sc8 = sb("q_sc8", [128, 128])
        q_rdq = sb("q_rdq", [128, 128])
        q_hsc = sb("q_hsc", [128, 128], BF16)
        q_pl_ref = [None]
        q_zs = sb("q_zs", [128, 128])
        q_u = sb("q_u", [128, 1024])
        q_z = sb("q_z", [128, 1024])
        q_p1 = sb("q_p1", [128, 1024], BF16)
        q_p2 = sb("q_p2", [128, 1024], BF16)
        q_sg = sb("q_sg", [128, 1024], BF16)
        q_ob = sb("q_ob", [128, 1024], BF16)
        zer = sb("zer", [128, 128])
        junk = sb("junkt", [128, 128])
        junk2 = sb("junk2", [128, 128])
        uc_parts = sb("uc_parts", [128, NF * NS])
        cs_parts = sb("cs_parts", [128, NF * NS])
        so_parts = sb("so_parts", [128, NF * NO])
        xst = [sb(f"xst{i}", [128, H]) for i in range(2)]
        xn_t = sb("xn_t", [128, H])
        xnb = [sb(f"xnb{i}", [128, H], BF16) for i in range(2)]
        xnTo = [sb(f"xnTo{i}", [128, NF * 128], BF16) for i in range(2)]

        PS = [es.enter_context(nc.psum_tensor(f"ps{i}", [128, 512], F32))
              for i in range(8)]

        for s in ["ld", "tp", "sp", "v", "s", "t"]:
            C.sem(s)

        bank_free = {i: ("v", 0) for i in range(8)}
        bank_fill = {i: 0 for i in range(8)}

        def mm_group(bank, emit_mms):
            sem, val = bank_free[bank]
            C.wait("tensor", sem, val)
            emit_mms()
            bank_fill[bank] = C.c("t")

        def consume_bank(bank, eng="vector"):
            C.wait(eng, "t", bank_fill[bank])

        def release_bank(bank, sem):
            bank_free[bank] = (sem, C.c(sem))

        def sep():
            C.op("vector", lambda E: E.tensor_scalar_mul(
                out=junk2[:, :], in0=junk[:, :], scalar1=1.0))

        ORDER = ["P2", "P4", "P3", "P5", "F1", "F2"]

        def _ph(p):
            return upto is None or ORDER.index(p) <= ORDER.index(upto)

        for a, t in [(bd_h, bdv), (a1_h, a1v), (b1_h, b1v), (dvec_h, dv),
                     (sin_h, sinv), (sind_h, sindv), (gate_h, gatev),
                     (ident_h, identv), (ones1_h, ones1v)]:
            C.op("sync", lambda E, a=a, t=t: E.dma_start(out=t[:, :], in_=a),
                 inc="ld", by=16)
        for buf, val in [(junk, 1.0), (junk2, 1.0), (zer, 0.0), (gmx_row, 1.0)]:
            C.op("vector", lambda E, b=buf, v=val: E.memset(b[:, :], v))
        for buf in [uc_parts, cs_parts, so_parts, gmx_run]:
            C.op("vector", lambda E, b=buf: E.memset(b[:, :], 0.0))
        C.op("vector", lambda E: E.tensor_tensor_scan(
            out=junk[:, 0:64], data0=junk[:, 0:64], data1=junk[:, 64:128],
            initial=0.0, op0=ALU.add, op1=ALU.add))
        C.op("vector", lambda E: E.tensor_reduce(
            out=junk[:, 0:8], in_=junk[:, 0:128].rearrange("p (n k) -> p n k", k=16),
            axis=AX.X, op=ALU.max, apply_absolute_value=True))
        C.op("vector", lambda E: E.reciprocal(out=junk[:, 0:64], in_=junk[:, 64:128]))
        C.op("vector", lambda E: E.memset(junk[:, :], 1.0), inc="v")
        ld_consts = C.c("ld")

        for i in range(NT):
            C.op("sync", lambda E, i=i: E.dma_start(
                out=acc[:, i * H:(i + 1) * H],
                in_=x_own[i * 128:(i + 1) * 128, :]), inc="ld", by=16)

        def emit_stats(src_ap, col):
            C.op("vector", lambda E, src_ap=src_ap, col=col: E.tensor_reduce(
                out=m_all[:, col:col + 1], in_=src_ap, axis=AX.X, op=ALU.add),
                inc="v")
            C.wait("scalar", "v")
            C.op("scalar", lambda E, src_ap=src_ap: E.activation(
                out=xn_t[:, :], in_=src_ap, func=AF.Square), inc="s")
            C.wait("vector", "s")
            C.op("vector", lambda E, col=col: E.tensor_reduce(
                out=r_all[:, col:col + 1], in_=xn_t[:, :], axis=AX.X, op=ALU.add),
                inc="v")

        for i in range(NT):
            C.wait("vector", "ld", ld_consts + 16 * (i + 1))
            emit_stats(acc[:, i * H:(i + 1) * H], i)
        for i in range(NO):
            b = i % 2
            C.wait("sync", "s")
            C.op("sync", lambda E, i=i, b=b: E.dma_start(
                out=xst[b][:, :], in_=x_oth[i * 128:(i + 1) * 128, :]),
                inc="ld", by=16)
            C.wait("vector", "ld")
            emit_stats(xst[b][:, :], NT + i)

        C.op("vector", lambda E: E.tensor_scalar_mul(
            out=m_all[:, 0:NS], in0=m_all[:, 0:NS], scalar1=float(1.0 / H)))
        C.op("vector", lambda E: E.tensor_scalar_mul(
            out=r_all[:, 0:NS], in0=r_all[:, 0:NS], scalar1=float(1.0 / H)))
        sep()
        C.op("vector", lambda E: E.tensor_tensor(
            out=q_t0[:, 0:NS], in0=m_all[:, 0:NS], in1=m_all[:, 0:NS], op=ALU.mult))
        sep()
        C.op("vector", lambda E: E.tensor_sub(
            out=r_all[:, 0:NS], in0=r_all[:, 0:NS], in1=q_t0[:, 0:NS]))
        sep()
        C.op("vector", lambda E: E.tensor_scalar_add(
            out=r_all[:, 0:NS], in0=r_all[:, 0:NS], scalar1=1e-5), inc="v")
        C.wait("scalar", "v")
        C.op("scalar", lambda E: E.activation(
            out=q_zs[:, 0:NS], in_=r_all[:, 0:NS], func=AF.Sqrt), inc="s")
        C.wait("vector", "s")
        C.op("vector", lambda E: E.reciprocal(out=q_t0[:, 0:NS], in_=q_zs[:, 0:NS]))
        sep()
        C.op("vector", lambda E: E.tensor_tensor(
            out=q_t0[:, 0:NS], in0=r_all[:, 0:NS], in1=q_t0[:, 0:NS], op=ALU.mult))
        sep()
        C.op("vector", lambda E: E.tensor_scalar_mul(
            out=q_t0[:, 0:NS], in0=q_t0[:, 0:NS], scalar1=0.5))
        sep()
        C.op("vector", lambda E: E.scalar_tensor_tensor(
            out=q_t0[:, 0:NS], in0=q_zs[:, 0:NS], scalar=0.5, in1=q_t0[:, 0:NS],
            op0=ALU.mult, op1=ALU.add))
        sep()
        C.op("vector", lambda E: E.reciprocal(out=r_all[:, 0:NS], in_=q_t0[:, 0:NS]))
        sep()
        sep()
        C.op("vector", lambda E: E.tensor_scalar_mul(
            out=junk2[:, :], in0=junk[:, :], scalar1=1.0), inc="v")
        if dbg:
            C.wait("sync", "v")
            C.op("sync", lambda E: E.dma_start(
                out=dbg_h["d_stats"][:, 0:NS], in_=m_all[:, 0:NS]), inc="sp", by=16)
            C.op("sync", lambda E: E.dma_start(
                out=dbg_h["d_stats"][:, NS:2 * NS], in_=r_all[:, 0:NS]),
                inc="sp", by=16)

        def emit_xn_tile(src_ap, col, out_f32, out_bf16):
            if out_f32 is not None:
                C.op("vector", lambda E: E.tensor_scalar(
                    out=out_f32, in0=src_ap,
                    scalar1=m_all[:, col:col + 1], scalar2=r_all[:, col:col + 1],
                    op0=ALU.subtract, op1=ALU.mult))
                C.op("vector", lambda E: E.tensor_copy(out=out_bf16, in_=out_f32),
                     inc="v")
            else:
                C.op("vector", lambda E: E.tensor_scalar(
                    out=out_bf16, in0=src_ap,
                    scalar1=m_all[:, col:col + 1], scalar2=r_all[:, col:col + 1],
                    op0=ALU.subtract, op1=ALU.mult), inc="v")

        def emit_tr8(src_bf16, dst_cols):
            C.wait("scalar", "v")
            for j in range(NF):
                C.op("scalar", lambda E, j=j: E.dma_start_transpose(
                    out=dst_cols(j), in_=src_bf16[:, j * 128:(j + 1) * 128]),
                    inc="tp", by=16)

        def stream_tile(gi):
            b = gi % 2
            if gi < NT:
                C.wait("vector", "tp")
                emit_xn_tile(acc[:, gi * H:(gi + 1) * H], gi, None, xnb[b][:, :])
            else:
                i = gi - NT
                C.wait("sync", "v")
                C.op("sync", lambda E, i=i, b=b: E.dma_start(
                    out=xst[b][:, :], in_=x_oth[i * 128:(i + 1) * 128, :]),
                    inc="ld", by=16)
                C.wait("vector", "ld")
                C.wait("vector", "tp")
                emit_xn_tile(xst[b][:, :], gi, None, xnb[b][:, :])
            C.wait("scalar", "t")
            emit_tr8(xnb[b], lambda j, b=b: xnTo[b][:, j * 128:(j + 1) * 128])
            C.wait("tensor", "tp")
            return b

        esA1 = ExitStack()
        if noreuse:
            es.enter_context(esA1)
        with esA1:
            wcT = sb("wcT", [128, NF * H], BF16, esA1)
            wldA = sb("wld", [128, H], st=esA1)
            wldbA = sb("wldb", [128, H], BF16, st=esA1)

            def emit_wT(w_in, wT):
                for fi in range(NF):
                    C.wait("sync", "v")
                    C.op("sync", lambda E, fi=fi, w_in=w_in, wld=wldA:
                         E.dma_start(
                             out=wld[:, :], in_=w_in[fi * 128:(fi + 1) * 128, :]),
                         inc="ld", by=16)
                    C.wait("vector", "ld")
                    C.wait("vector", "tp")
                    C.op("vector", lambda E, wld=wldA, wldb=wldbA: E.tensor_copy(
                        out=wldb[:, :], in_=wld[:, :]), inc="v")
                    emit_tr8(wldbA, lambda j, fi=fi, wT=wT:
                             wT[:, j * H + fi * 128: j * H + (fi + 1) * 128])

            emit_wT(wc_h, wcT)
            C.wait("tensor", "tp")

            for gi in range(NS):
                b = stream_tile(gi)

                def rhs(j, b=b):
                    return xnTo[b][:, j * 128:(j + 1) * 128]

                for fi in range(NF):
                    bank = fi % 2

                    def mms(fi=fi, bank=bank, rhs=rhs):
                        for j in range(NF):
                            C.op("tensor", lambda E, fi=fi, j=j, bank=bank,
                                 rhs=rhs: E.matmul(
                                     PS[bank][:, 0:128],
                                     wcT[:, j * H + fi * 128:
                                         j * H + (fi + 1) * 128],
                                     rhs(j),
                                     start=(j == 0), stop=(j == NF - 1)),
                                 inc=("t" if j == NF - 1 else None))
                    mm_group(bank, mms)
                    consume_bank(bank)
                    if dbg and gi == 0:
                        C.op("vector", lambda E, bank=bank: E.tensor_copy(
                            out=q_zs[:, :], in_=PS[bank][:, 0:128]), inc="v")
                        C.wait("sync", "v")
                        C.op("sync", lambda E, fi=fi: E.dma_start(
                            out=dbg_h["d_cs0"][fi * 128:(fi + 1) * 128, :],
                            in_=q_zs[:, :]), inc="sp", by=16)
                        C.wait("vector", "sp")
                        if fi == 0:
                            for j in range(NF):
                                C.op("vector", lambda E, j=j, rhs=rhs:
                                     E.tensor_copy(out=q_zs[:, :],
                                                   in_=rhs(j)), inc="v")
                                C.wait("sync", "v")
                                C.op("sync", lambda E, j=j: E.dma_start(
                                    out=dbg_h["d_xnto"][j * 128:(j + 1) * 128, :],
                                    in_=q_zs[:, :]), inc="sp", by=16)
                                C.wait("vector", "sp")
                                C.op("vector", lambda E, j=j: E.tensor_copy(
                                    out=q_zs[:, :],
                                    in_=wcT[:, j * H:j * H + 128]), inc="v")
                                C.wait("sync", "v")
                                C.op("sync", lambda E, j=j: E.dma_start(
                                    out=dbg_h["d_wct"][j * 128:(j + 1) * 128, :],
                                    in_=q_zs[:, :]), inc="sp", by=16)
                                C.wait("vector", "sp")
                    C.op("vector", lambda E, fi=fi, bank=bank, gi=gi, rhs=rhs:
                         E.scalar_tensor_tensor(
                             out=q_z[:, 0:128], in0=rhs(fi), scalar=1.0,
                             in1=PS[bank][:, 0:128], op0=ALU.mult, op1=ALU.mult,
                             accum_out=uc_parts[:, fi * NS + gi:
                                                fi * NS + gi + 1]))
                    C.op("vector", lambda E, fi=fi, bank=bank, gi=gi:
                         E.tensor_reduce(
                             out=cs_parts[:, fi * NS + gi:fi * NS + gi + 1],
                             in_=PS[bank][:, 0:128], axis=AX.X, op=ALU.add),
                         inc="v")
                    release_bank(bank, "v")

        if dbg:
            C.wait("sync", "v")
            C.op("sync", lambda E: E.dma_start(
                out=dbg_h["d_parts"][:, 0:NF * NS], in_=uc_parts[:, :]),
                inc="sp", by=16)
            C.op("sync", lambda E: E.dma_start(
                out=dbg_h["d_parts"][:, NF * NS:], in_=cs_parts[:, :]),
                inc="sp", by=16)
            C.wait("vector", "sp")
        for fi in range(NF):
            C.op("vector", lambda E, fi=fi: E.tensor_reduce(
                out=uc[:, fi:fi + 1], in_=uc_parts[:, fi * NS:(fi + 1) * NS],
                axis=AX.X, op=ALU.add))
            C.op("vector", lambda E, fi=fi: E.tensor_reduce(
                out=csum[:, fi:fi + 1], in_=cs_parts[:, fi * NS:(fi + 1) * NS],
                axis=AX.X, op=ALU.add))
        sep()
        sep()
        sep()
        C.op("vector", lambda E: E.tensor_tensor(
            out=b1uc[:, :], in0=b1v[:, :], in1=uc[:, :], op=ALU.mult), inc="v")
        if dbg:
            C.wait("sync", "v")
            C.op("sync", lambda E: E.dma_start(
                out=dbg_h["d_ucs"][:, 0:NF], in_=uc[:, :]), inc="sp", by=16)
            C.op("sync", lambda E: E.dma_start(
                out=dbg_h["d_ucs"][:, NF:2 * NF], in_=csum[:, :]), inc="sp", by=16)

        def emit_block_stage(nb, gs_ap, rgs_ap):
            pl = q_pl_ref[0]
            sep()
            C.op("vector", lambda E, gs_ap=gs_ap, nb=nb: E.tensor_scalar(
                out=q_sc[:, 0:nb], in0=q_amax[:, 0:nb], scalar1=gs_ap,
                scalar2=2688.0, op0=ALU.mult, op1=ALU.min))
            sep()
            for k in range(-5, 9):
                i = k + 5
                C.op("vector", lambda E, k=k, i=i, nb=nb, pl=pl: E.tensor_scalar(
                    out=pl[:, i * nb:(i + 1) * nb], in0=q_sc[:, 0:nb],
                    scalar1=float(6.0 * 2.0 ** k), scalar2=1.0,
                    op0=ALU.is_ge, op1=ALU.add))
            sep()
            C.op("vector", lambda E, nb=nb, pl=pl: E.tensor_reduce(
                out=q_step[:, 0:nb],
                in_=pl[:, 0:14 * nb].rearrange("p (k n) -> p n k", n=nb),
                axis=AX.X, op=ALU.mult))
            sep()
            C.op("vector", lambda E, nb=nb: E.tensor_scalar_mul(
                out=q_step[:, 0:nb], in0=q_step[:, 0:nb],
                scalar1=float(2.0 ** -9)))
            sep()
            C.op("vector", lambda E, nb=nb: E.reciprocal(
                out=q_t0[:, 0:nb], in_=q_step[:, 0:nb]))
            sep()
            sep()
            C.op("vector", lambda E, nb=nb: E.tensor_tensor(
                out=q_t0[:, 0:nb], in0=q_sc[:, 0:nb], in1=q_t0[:, 0:nb],
                op=ALU.mult))
            sep()
            C.op("vector", lambda E, nb=nb: E.tensor_single_scalar(
                out=q_mi[:, 0:nb], in_=q_t0[:, 0:nb], scalar=3.0, op=ALU.is_ge))
            for j in range(1, 16):
                C.op("vector", lambda E, j=j, nb=nb: E.scalar_tensor_tensor(
                    out=q_mi[:, 0:nb], in0=q_t0[:, 0:nb],
                    scalar=float(3 * (2 * j + 1)),
                    in1=q_mi[:, 0:nb], op0=ALU.is_ge, op1=ALU.add))
            sep()
            C.op("vector", lambda E, nb=nb: E.tensor_tensor(
                out=q_sc8[:, 0:nb], in0=q_mi[:, 0:nb], in1=q_step[:, 0:nb],
                op=ALU.mult))
            sep()
            C.op("vector", lambda E, rgs_ap=rgs_ap, nb=nb: E.tensor_single_scalar(
                out=q_t0[:, 0:nb], in_=q_sc8[:, 0:nb], scalar=rgs_ap,
                op=ALU.mult))
            sep()
            C.op("vector", lambda E, nb=nb: E.scalar_tensor_tensor(
                out=q_t0[:, 0:nb], in0=q_t0[:, 0:nb], scalar=0.0,
                in1=q_t0[:, 0:nb], op0=ALU.is_equal, op1=ALU.add))
            sep()
            C.op("vector", lambda E, nb=nb: E.reciprocal(
                out=q_rdq[:, 0:nb], in_=q_t0[:, 0:nb]))
            C.op("vector", lambda E, nb=nb: E.tensor_scalar_mul(
                out=q_hsc[:, 0:nb], in0=q_sc8[:, 0:nb], scalar1=0.5))
            sep()
            sep()
            C.op("vector", lambda E: E.tensor_scalar_mul(
                out=junk2[:, :], in0=junk[:, :], scalar1=1.0), inc="v")

        def emit_elem_stage(src_f32, nb, out_bf16, amax_off=0):
            F = 16 * nb
            rdq_b = q_rdq[:, amax_off:amax_off + nb].unsqueeze(2).broadcast_to(
                [128, nb, 16])
            hsc_b = q_hsc[:, amax_off:amax_off + nb].unsqueeze(2).broadcast_to(
                [128, nb, 16])

            def v3(ap):
                return ap.rearrange("p (n k) -> p n k", k=16)

            C.op("vector", lambda E: E.tensor_tensor(
                out=v3(q_u[:, 0:F]), in0=v3(src_f32), in1=rdq_b, op=ALU.mult),
                inc="v")
            C.wait("scalar", "v")
            C.op("scalar", lambda E: E.activation(
                out=q_z[:, 0:F], in_=q_u[:, 0:F], func=AF.Square))
            C.op("scalar", lambda E: E.activation(
                out=q_sg[:, 0:F], in_=q_u[:, 0:F], func=AF.Sign), inc="s")
            C.wait("vector", "s")
            C.op("vector", lambda E: E.tensor_single_scalar(
                out=q_p1[:, 0:F], in_=q_z[:, 0:F], scalar=0.0625, op=ALU.is_ge))
            for thr in (0.5625, 1.5625, 3.0625):
                C.op("vector", lambda E, thr=thr: E.scalar_tensor_tensor(
                    out=q_p1[:, 0:F], in0=q_z[:, 0:F], scalar=thr,
                    in1=q_p1[:, 0:F], op0=ALU.is_ge, op1=ALU.add))
            C.op("vector", lambda E: E.tensor_single_scalar(
                out=q_p2[:, 0:F], in_=q_z[:, 0:F], scalar=6.25, op=ALU.is_ge))
            for thr in (12.25, 25.0, 25.0):
                C.op("vector", lambda E, thr=thr: E.scalar_tensor_tensor(
                    out=q_p2[:, 0:F], in0=q_z[:, 0:F], scalar=thr,
                    in1=q_p2[:, 0:F], op0=ALU.is_ge, op1=ALU.add))
            C.op("vector", lambda E: E.scalar_tensor_tensor(
                out=q_p1[:, 0:F], in0=q_p2[:, 0:F], scalar=2.0, in1=q_p1[:, 0:F],
                op0=ALU.mult, op1=ALU.add))
            C.op("vector", lambda E: E.tensor_mul(
                out=q_p1[:, 0:F], in0=q_p1[:, 0:F], in1=q_sg[:, 0:F]))
            C.op("vector", lambda E: E.tensor_tensor(
                out=v3(out_bf16), in0=v3(q_p1[:, 0:F]), in1=hsc_b, op=ALU.mult),
                inc="v")

        def emit_amax(src_f32, nb, off=0):
            C.op("vector", lambda E: E.tensor_reduce(
                out=q_amax[:, off:off + nb],
                in_=src_f32.rearrange("p (n k) -> p n k", k=16),
                axis=AX.X, op=ALU.max, apply_absolute_value=True), inc="v")

        if _ph("P4"):
            esP4 = ExitStack()
            q_pl_ref[0] = sb("q_pl4", [128, 14 * 64], st=esP4)
            C.op("vector", lambda E: E.reciprocal(out=rgs_x[:, :], in_=sinv[:, :]))
            C.op("vector", lambda E: E.reciprocal(out=rgs_hd[:, :], in_=sindv[:, :]))
            sep()
            sep()
            for i in range(NT):
                C.wait("vector", "tp")
                emit_xn_tile(acc[:, i * H:(i + 1) * H], i, xn_t[:, :], xnb[0][:, :])
                emit_amax(xn_t[:, :], 64)
                emit_block_stage(64, sinv[:, 0:1], rgs_x[:, 0:1])
                emit_elem_stage(xn_t[:, :], 64, q_ob[:, :])
                emit_tr8(q_ob, lambda j, i=i: xqT[:, j * T + i * 128:
                                                 j * T + (i + 1) * 128])
                if dbg:
                    C.op("vector", lambda E: E.tensor_copy(
                        out=q_z[:, :], in_=q_ob[:, :]), inc="v")
                    C.wait("sync", "v")
                    C.op("sync", lambda E, i=i: E.dma_start(
                        out=dbg_h["d_xq"][i * 128:(i + 1) * 128, :],
                        in_=q_z[:, :]), inc="sp", by=16)
                    C.wait("vector", "sp")
            if dbg:
                for i in range(NT):
                    emit_xn_tile(acc[:, i * H:(i + 1) * H], i, xn_t[:, :],
                                 xnb[0][:, :])
                    C.wait("sync", "v")
                    C.op("sync", lambda E, i=i: E.dma_start(
                        out=dbg_h["d_xn"][i * 128:(i + 1) * 128, :],
                        in_=xn_t[:, :]), inc="sp", by=16)
                    C.wait("vector", "sp")
            esP4.close()
            q_pl_ref[0] = None

        if _ph("P3"):
            esA2 = ExitStack()
            with esA2:
                wdpT = sb("wdpT", [128, NF * H], BF16, esA2)
                wdlT = sb("wdlT", [128, NF * H], BF16, esA2)
                wresT = sb("wresT", [128, NF * H], BF16, esA2)
                wldA2 = sb("wld2w", [128, H], st=esA2)
                wldbA2 = sb("wldb2", [128, H], BF16, st=esA2)
                dinT = sb("dinT", [128, NF * 128], BF16, st=esA2)
                at_d = sb("at_d", [128, 128], st=esA2)
                at_e = sb("at_e", [128, 128], st=esA2)
                at_s = sb("at_s", [128, 128], st=esA2)
                at_c = [sb(f"at_c{i}", [128, 128], st=esA2) for i in range(2)]
                at_y = sb("at_y", [128, 128], st=esA2)

                def emit_wT2(w_in, wT):
                    for fi in range(NF):
                        C.wait("sync", "v")
                        C.op("sync", lambda E, fi=fi, w_in=w_in, wld=wldA2:
                             E.dma_start(
                                 out=wld[:, :],
                                 in_=w_in[fi * 128:(fi + 1) * 128, :]),
                             inc="ld", by=16)
                        C.wait("vector", "ld")
                        C.wait("vector", "tp")
                        C.op("vector", lambda E, wld=wldA2, wldb=wldbA2:
                             E.tensor_copy(
                                 out=wldb[:, :], in_=wld[:, :]), inc="v")
                        emit_tr8(wldbA2, lambda j, fi=fi, wT=wT:
                                 wT[:, j * H + fi * 128: j * H + (fi + 1) * 128])

                emit_wT2(wdp_h, wdpT)
                emit_wT2(wdl_h, wdlT)
                emit_wT2(wres_h, wresT)
                C.wait("tensor", "tp")

                def delta_for_tile(b, out_cb):
                    C.wait("vector", "t")
                    for fi in range(NF):
                        bank = fi % 2

                        def mms(fi=fi, bank=bank, b=b):
                            for j in range(NF):
                                C.op("tensor", lambda E, fi=fi, j=j, bank=bank, b=b:
                                     E.matmul(PS[bank][:, 0:128],
                                              wdpT[:, j * H + fi * 128:
                                                   j * H + (fi + 1) * 128],
                                              xnTo[b][:, j * 128:(j + 1) * 128],
                                              start=(j == 0), stop=(j == NF - 1)),
                                     inc=("t" if j == NF - 1 else None))
                        mm_group(bank, mms)
                        consume_bank(bank)
                        C.op("vector", lambda E, fi=fi, bank=bank: E.tensor_copy(
                            out=dinT[:, fi * 128:(fi + 1) * 128],
                            in_=PS[bank][:, 0:128]), inc="v")
                        release_bank(bank, "v")
                    C.wait("tensor", "v")
                    for fi in range(NF):
                        bank = 2 + (fi % 2)

                        def mms2(fi=fi, bank=bank):
                            for j in range(NF):
                                C.op("tensor", lambda E, fi=fi, j=j, bank=bank:
                                     E.matmul(PS[bank][:, 0:128],
                                              wdlT[:, j * H + fi * 128:
                                                   j * H + (fi + 1) * 128],
                                              dinT[:, j * 128:(j + 1) * 128],
                                              start=(j == 0), stop=(j == NF - 1)),
                                     inc=("t" if j == NF - 1 else None))
                        mm_group(bank, mms2)
                        consume_bank(bank, "scalar")
                        C.op("scalar", lambda E, fi=fi, bank=bank: E.activation(
                            out=at_e[:, :], in_=PS[bank][:, 0:128], func=AF.Exp,
                            bias=bdv[:, fi:fi + 1]), inc="s")
                        C.op("scalar", lambda E: E.activation(
                            out=at_d[:, :], in_=at_e[:, :], func=AF.Ln,
                            bias=1.0), inc="s")
                        release_bank(bank, "s")
                        out_cb(fi)

                for i in range(NO):
                    b = stream_tile(NT + i)

                    def oth_cb(fi, i=i):
                        C.wait("vector", "s")
                        C.op("vector", lambda E, fi=fi: E.tensor_single_scalar(
                            out=at_s[:, :], in_=at_d[:, :],
                            scalar=b1v[:, fi:fi + 1], op=ALU.mult))
                        C.op("vector", lambda E, fi=fi, i=i: E.tensor_reduce(
                            out=so_parts[:, fi * NO + i:fi * NO + i + 1],
                            in_=at_s[:, :], axis=AX.X, op=ALU.add), inc="v")

                    delta_for_tile(b, oth_cb)

                C.wait("vector", "s")
                for fi in range(NF):
                    C.op("vector", lambda E, fi=fi: E.tensor_reduce(
                        out=s_oth[:, fi:fi + 1],
                        in_=so_parts[:, fi * NO:(fi + 1) * NO],
                        axis=AX.X, op=ALU.add))
                sep()
                sep()
                sep()
                C.op("vector", lambda E: E.tensor_single_scalar(
                    out=carry[:, :], in_=s_oth[:, :], scalar=gatev[:, 0:1],
                    op=ALU.mult))
                sep()
                sep()
                C.op("vector", lambda E: E.tensor_scalar_mul(
                    out=junk2[:, :], in0=junk[:, :], scalar1=1.0), inc="v")
                if dbg:
                    C.wait("sync", "v")
                    C.op("sync", lambda E: E.dma_start(
                        out=dbg_h["d_att"][:, 0:NF], in_=s_oth[:, :]), inc="sp", by=16)
                    C.op("sync", lambda E: E.dma_start(
                        out=dbg_h["d_att"][:, NF:2 * NF], in_=carry[:, :]),
                        inc="sp", by=16)

                for i in range(NT):
                    b = stream_tile(i)
                    cb = i % 2

                    def own_cb(fi, i=i, cb=cb, b=b):
                        C.wait("vector", "s")
                        C.op("vector", lambda E, fi=fi: E.tensor_single_scalar(
                            out=at_s[:, :], in_=at_d[:, :],
                            scalar=b1v[:, fi:fi + 1], op=ALU.mult))
                        C.op("vector", lambda E, fi=fi, cb=cb: E.tensor_tensor_scan(
                            out=at_c[cb][:, :], data0=at_s[:, :], data1=zer[:, :],
                            initial=carry[:, fi:fi + 1], op0=ALU.add, op1=ALU.add))
                        C.wait("scalar", "v", C.c("v"))
                        C.op("scalar", lambda E, fi=fi: E.activation(
                            out=at_e[:, :], in_=at_d[:, :], func=AF.Exp,
                            scale=a1v[:, fi:fi + 1]), inc="s")
                        C.wait("vector", "s")
                        C.op("vector", lambda E, fi=fi: E.tensor_single_scalar(
                            out=at_y[:, :], in_=at_e[:, :],
                            scalar=b1uc[:, fi:fi + 1], op=ALU.mult))
                        C.op("vector", lambda E, fi=fi, cb=cb: E.scalar_tensor_tensor(
                            out=at_y[:, :], in0=at_c[cb][:, :],
                            scalar=csum[:, fi:fi + 1], in1=at_y[:, :],
                            op0=ALU.mult, op1=ALU.add), inc="v")
                        C.op("vector", lambda E, fi=fi, cb=cb: E.tensor_copy(
                            out=carry[:, fi:fi + 1], in_=at_c[cb][:, 127:128]))
                        rbank = 4 + (fi % 2)

                        def mmsr(fi=fi, rbank=rbank, b=b):
                            for j in range(NF):
                                C.op("tensor", lambda E, fi=fi, j=j, rbank=rbank, b=b:
                                     E.matmul(PS[rbank][:, 0:128],
                                              wresT[:, j * H + fi * 128:
                                                    j * H + (fi + 1) * 128],
                                              xnTo[b][:, j * 128:(j + 1) * 128],
                                              start=(j == 0), stop=(j == NF - 1)),
                                     inc=("t" if j == NF - 1 else None))
                        mm_group(rbank, mmsr)
                        consume_bank(rbank)
                        C.op("vector", lambda E, rbank=rbank: E.scalar_tensor_tensor(
                            out=at_y[:, :], in0=PS[rbank][:, 0:128],
                            scalar=dv[:, 0:1], in1=at_y[:, :],
                            op0=ALU.mult, op1=ALU.add), inc="v")
                        release_bank(rbank, "v")
                        tbank = 6 + (fi % 2)
                        sem, val = bank_free[tbank]
                        C.wait("tensor", sem, val)
                        C.wait("tensor", "v", C.c("v"))
                        C.op("tensor", lambda E, tbank=tbank: E.transpose(
                            PS[tbank][:, 0:128], at_y[:, :], identv[:, :]), inc="t")
                        bank_fill[tbank] = C.c("t")
                        consume_bank(tbank)
                        C.op("vector", lambda E, i=i, fi=fi, tbank=tbank: E.tensor_add(
                            out=acc[:, i * H + fi * 128:i * H + (fi + 1) * 128],
                            in0=acc[:, i * H + fi * 128:i * H + (fi + 1) * 128],
                            in1=PS[tbank][:, 0:128]), inc="v")
                        release_bank(tbank, "v")

                    delta_for_tile(b, own_cb)

            if dbg:
                C.wait("sync", "v")
                for i in range(NT):
                    C.op("sync", lambda E, i=i: E.dma_start(
                        out=dbg_h["d_yat"][i * 128:(i + 1) * 128, :],
                        in_=acc[:, i * H:(i + 1) * H]), inc="sp", by=16)

        if _ph("P5"):
            esB = ExitStack()
            with esB:
                wld2 = sb("wld2", [128, II], st=esB)
                q_pl_ref[0] = sb("q_pl5", [128, 14 * 64], st=esB)

                def gmax_pass(w_in, ntiles, wcols, wi):
                    for ti in range(ntiles):
                        C.wait("sync", "v")
                        C.op("sync", lambda E, ti=ti, w_in=w_in, wcols=wcols:
                             E.dma_start(out=wld2[:, 0:wcols],
                                         in_=w_in[ti * 128:(ti + 1) * 128, :]),
                             inc="ld", by=16)
                        C.wait("vector", "ld")
                        C.op("vector", lambda E, ti=ti, wi=wi, wcols=wcols:
                             E.tensor_reduce(
                                 out=gmx_parts[:, wi * NI + ti:wi * NI + ti + 1],
                                 in_=wld2[:, 0:wcols], axis=AX.X,
                                 op=ALU.max, apply_absolute_value=True), inc="v")

                gmax_pass(gw_h, NI, H, 0)
                gmax_pass(uw_h, NI, H, 1)
                gmax_pass(dw_h, NF, II, 2)
                sep()
                sep()
                sep()
                for wi in range(3):
                    C.op("vector", lambda E, wi=wi: E.tensor_reduce(
                        out=gmx_run[:, wi:wi + 1],
                        in_=gmx_parts[:, wi * NI:wi * NI + (NI if wi < 2 else NF)],
                        axis=AX.X, op=ALU.max))
                sep()
                sep()
                C.op("vector", lambda E: E.tensor_scalar_mul(
                    out=junk2[:, :], in0=junk[:, :], scalar1=1.0), inc="v")

                sem, val = bank_free[0]
                C.wait("tensor", sem, val)
                C.wait("tensor", "v", C.c("v"))
                C.op("tensor", lambda E: E.transpose(
                    PS[0][0:3, 0:128], gmx_run[:, 0:3], identv[:, :]), inc="t")
                bank_fill[0] = C.c("t")
                consume_bank(0)
                C.op("vector", lambda E: E.tensor_reduce(
                    out=gmx1[0:3, 0:1], in_=PS[0][0:3, 0:128], axis=AX.X, op=ALU.max),
                    inc="v")
                release_bank(0, "v")
                sem, val = bank_free[1]
                C.wait("tensor", sem, val)
                C.wait("tensor", "v", C.c("v"))
                C.op("tensor", lambda E: E.transpose(
                    PS[1][0:1, 0:3], gmx1[0:3, 0:1], identv[0:3, 0:3]), inc="t")
                bank_fill[1] = C.c("t")
                consume_bank(1)
                C.op("vector", lambda E: E.tensor_copy(
                    out=gmx_row[0:1, 0:3], in_=PS[1][0:1, 0:3]), inc="v")
                release_bank(1, "v")
                sem, val = bank_free[2]
                C.wait("tensor", sem, val)
                C.wait("tensor", "v", C.c("v"))
                C.op("tensor", lambda E: E.matmul(
                    PS[2][:, 0:128], ones1v[0:1, :], gmx_row[0:1, 0:128],
                    start=True, stop=True), inc="t")
                bank_fill[2] = C.c("t")
                consume_bank(2)
                C.op("vector", lambda E: E.tensor_scalar_mul(
                    out=q_zs[:, :], in0=PS[2][:, 0:128], scalar1=1.0))
                sep()
                C.op("vector", lambda E: E.reciprocal(
                    out=gsw[:, :], in_=q_zs[:, :]))
                sep()
                sep()
                C.op("vector", lambda E: E.tensor_scalar_mul(
                    out=gsw[:, :], in0=gsw[:, :], scalar1=2688.0))
                sep()
                C.op("vector", lambda E: E.reciprocal(
                    out=rgsw[:, :], in_=gsw[:, :]))
                sep()
                sep()
                C.op("vector", lambda E: E.tensor_tensor(
                    out=scale_g[:, :], in0=rgs_x[:, :], in1=rgsw[:, 0:1], op=ALU.mult))
                C.op("vector", lambda E: E.tensor_tensor(
                    out=scale_u[:, :], in0=rgs_x[:, :], in1=rgsw[:, 1:2], op=ALU.mult))
                C.op("vector", lambda E: E.tensor_tensor(
                    out=scale_d[:, :], in0=rgs_hd[:, :], in1=rgsw[:, 2:3], op=ALU.mult),
                    inc="v")
                release_bank(2, "v")

                def wquant_pass(w_in, s_out, ntiles, wcols, wi):
                    nsub = wcols // 1024
                    for ti in range(ntiles):
                        C.wait("sync", "v")
                        C.op("sync", lambda E, ti=ti, w_in=w_in, wcols=wcols:
                             E.dma_start(out=wld2[:, 0:wcols],
                                         in_=w_in[ti * 128:(ti + 1) * 128, :]),
                             inc="ld", by=16)
                        C.wait("vector", "ld")
                        for su in range(nsub):
                            src = wld2[:, su * 1024:(su + 1) * 1024]
                            emit_amax(src, 64)
                            emit_block_stage(64, gsw[:, wi:wi + 1],
                                             rgsw[:, wi:wi + 1])
                            emit_elem_stage(src, 64, q_ob[:, :])
                            C.wait("sync", "v")
                            C.op("sync", lambda E, ti=ti, su=su, s_out=s_out:
                                 E.dma_start(
                                     out=s_out[ti * 128:(ti + 1) * 128,
                                               su * 1024:(su + 1) * 1024],
                                     in_=q_ob[:, :]), inc="sp", by=16)
                            C.wait("vector", "sp")
                            if dbg and s_out is s_gq:
                                C.op("vector", lambda E: E.tensor_copy(
                                    out=q_z[:, :], in_=q_ob[:, :]), inc="v")
                                C.wait("sync", "v")
                                C.op("sync", lambda E, ti=ti: E.dma_start(
                                    out=dbg_h["d_gq"][ti * 128:(ti + 1) * 128, :],
                                    in_=q_z[:, :]), inc="sp", by=16)
                                C.wait("vector", "sp")

                wquant_pass(gw_h, s_gq, NI, H, 0)
                wquant_pass(uw_h, s_uq, NI, H, 1)
                wquant_pass(dw_h, s_dq, NF, II, 2)

        if _ph("F1"):
            esC = ExitStack()
            with esC:
                q_pl_ref[0] = sb("q_plF", [128, 14 * 128], st=esC)
                wgq = [sb(f"wgq{i}", [128, NF * 512], BF16, esC) for i in range(2)]
                wuq = [sb(f"wuq{i}", [128, NF * 512], BF16, esC) for i in range(2)]
                hid4 = sb("hid4", [128, 2048], st=esC)
                sgt = sb("sgt", [128, 512], st=esC)
                wbuf_free_t = [0, 0]

                for ic in range(II // 512):
                    wb = ic % 2
                    C.wait("scalar", "t", wbuf_free_t[wb])
                    for j in range(NF):
                        C.op("scalar", lambda E, ic=ic, j=j, wb=wb:
                             E.dma_start_transpose(
                                 out=wgq[wb][:, j * 512:(j + 1) * 512],
                                 in_=s_gq[ic * 512:(ic + 1) * 512,
                                          j * 128:(j + 1) * 128]), inc="tp", by=16)
                        C.op("scalar", lambda E, ic=ic, j=j, wb=wb:
                             E.dma_start_transpose(
                                 out=wuq[wb][:, j * 512:(j + 1) * 512],
                                 in_=s_uq[ic * 512:(ic + 1) * 512,
                                          j * 128:(j + 1) * 128]), inc="tp", by=16)
                    C.wait("tensor", "tp")
                    for ttg in range(NT // 4):
                        for k in range(4):
                            tt = ttg * 4 + k
                            gb, ub = 2 * k, 2 * k + 1
                            semg, valg = bank_free[gb]
                            C.wait("tensor", semg, valg)
                            semu, valu = bank_free[ub]
                            C.wait("tensor", semu, valu)
                            for j in range(NF):
                                C.op("tensor", lambda E, tt=tt, j=j, gb=gb, wb=wb:
                                     E.matmul(PS[gb][:, :],
                                              xqT[:, j * T + tt * 128:
                                                  j * T + (tt + 1) * 128],
                                              wgq[wb][:, j * 512:(j + 1) * 512],
                                              start=(j == 0), stop=(j == NF - 1)),
                                     inc=("t" if j == NF - 1 else None))
                            for j in range(NF):
                                C.op("tensor", lambda E, tt=tt, j=j, ub=ub, wb=wb:
                                     E.matmul(PS[ub][:, :],
                                              xqT[:, j * T + tt * 128:
                                                  j * T + (tt + 1) * 128],
                                              wuq[wb][:, j * 512:(j + 1) * 512],
                                              start=(j == 0), stop=(j == NF - 1)),
                                     inc=("t" if j == NF - 1 else None))
                            bank_fill[gb] = bank_fill[ub] = C.c("t")
                        for k in range(4):
                            gb, ub = 2 * k, 2 * k + 1
                            consume_bank(gb, "scalar")
                            C.wait("scalar", "v")
                            C.op("scalar", lambda E, gb=gb: E.activation(
                                out=sgt[:, :], in_=PS[gb][:, :], func=AF.Silu,
                                scale=scale_g[:, 0:1]), inc="s")
                            C.wait("vector", "s")
                            consume_bank(ub)
                            C.op("vector", lambda E, ub=ub, k=k: E.scalar_tensor_tensor(
                                out=hid4[:, k * 512:(k + 1) * 512],
                                in0=PS[ub][:, :], scalar=scale_u[:, 0:1],
                                in1=sgt[:, :], op0=ALU.mult, op1=ALU.mult), inc="v")
                            release_bank(gb, "v")
                            release_bank(ub, "v")
                        emit_amax(hid4[:, :], 128)
                        emit_block_stage(128, sindv[:, 0:1], rgs_hd[:, 0:1])
                        for half in range(2):
                            emit_elem_stage(hid4[:, half * 1024:(half + 1) * 1024],
                                            64, q_ob[:, :], amax_off=half * 64)
                            C.wait("sync", "v")
                            for k2 in range(2):
                                tt = ttg * 4 + half * 2 + k2
                                C.op("sync", lambda E, ic=ic, tt=tt, k2=k2:
                                     E.dma_start(
                                         out=s_hq[tt * 128:(tt + 1) * 128,
                                                  ic * 512:(ic + 1) * 512],
                                         in_=q_ob[:, k2 * 512:(k2 + 1) * 512]),
                                     inc="sp", by=16)
                            C.wait("vector", "sp")
                            if dbg:
                                C.op("vector", lambda E: E.tensor_copy(
                                    out=q_z[:, :], in_=q_ob[:, :]), inc="v")
                                C.wait("sync", "v")
                                for k2 in range(2):
                                    tt = ttg * 4 + half * 2 + k2
                                    C.op("sync", lambda E, ic=ic, tt=tt, k2=k2:
                                         E.dma_start(
                                             out=dbg_h["d_hq"][
                                                 tt * 128:(tt + 1) * 128,
                                                 ic * 512:(ic + 1) * 512],
                                             in_=q_z[:, k2 * 512:(k2 + 1) * 512]),
                                         inc="sp", by=16)
                                C.wait("vector", "sp")
                    wbuf_free_t[wb] = C.c("t")

        if _ph("F2"):
            esD = ExitStack()
            with esD:
                NH = NI // 2
                hqT = [sb(f"hqT{i}", [128, NH * 512], BF16, esD) for i in range(2)]
                wdqT = [sb(f"wdqT{i}", [128, H], BF16, esD) for i in range(2)]
                hbuf_free_t = [0, 0]
                wdq_free_t = [0, 0]
                for c in range(T // 512):
                    for half in range(2):
                        hb = (c * 2 + half) % 2
                        C.wait("scalar", "t", hbuf_free_t[hb])
                        for rr in range(NH):
                            r = half * NH + rr
                            C.op("scalar", lambda E, c=c, r=r, rr=rr, hb=hb:
                                 E.dma_start_transpose(
                                     out=hqT[hb][:, rr * 512:(rr + 1) * 512],
                                     in_=s_hq[c * 512:(c + 1) * 512,
                                              r * 128:(r + 1) * 128]),
                                 inc="tp", by=16)
                        C.wait("tensor", "tp")
                        if half == 0:
                            for bk in range(8):
                                sem, val = bank_free[bk]
                                C.wait("tensor", sem, val)
                        for rr in range(NH):
                            r = half * NH + rr
                            dwb = r % 2
                            C.wait("scalar", "t", wdq_free_t[dwb])
                            C.op("scalar", lambda E, r=r, dwb=dwb:
                                 E.dma_start_transpose(
                                     out=wdqT[dwb][:, 0:512],
                                     in_=s_dq[0:512, r * 128:(r + 1) * 128]),
                                 inc="tp", by=16)
                            C.op("scalar", lambda E, r=r, dwb=dwb:
                                 E.dma_start_transpose(
                                     out=wdqT[dwb][:, 512:1024],
                                     in_=s_dq[512:1024, r * 128:(r + 1) * 128]),
                                 inc="tp", by=16)
                            C.wait("tensor", "tp")
                            for tt in range(4):
                                for hh in range(2):
                                    C.op("tensor", lambda E, r=r, rr=rr, tt=tt,
                                         hh=hh, hb=hb, dwb=dwb: E.matmul(
                                             PS[tt * 2 + hh][:, :],
                                             hqT[hb][:, rr * 512 + tt * 128:
                                                     rr * 512 + (tt + 1) * 128],
                                             wdqT[dwb][:, hh * 512:(hh + 1) * 512],
                                             start=(r == 0), stop=(r == NI - 1)),
                                         inc=("t" if (hh == 1 and tt == 3)
                                              else None))
                            wdq_free_t[dwb] = C.c("t")
                        hbuf_free_t[hb] = C.c("t")
                    for bk in range(8):
                        bank_fill[bk] = C.c("t")
                    for tt in range(4):
                        for hh in range(2):
                            bk = tt * 2 + hh
                            consume_bank(bk)
                            C.op("vector", lambda E, c=c, tt=tt, hh=hh, bk=bk:
                                 E.scalar_tensor_tensor(
                                     out=acc[:, (c * 4 + tt) * H + hh * 512:
                                             (c * 4 + tt) * H + (hh + 1) * 512],
                                     in0=PS[bk][:, :], scalar=scale_d[:, 0:1],
                                     in1=acc[:, (c * 4 + tt) * H + hh * 512:
                                             (c * 4 + tt) * H + (hh + 1) * 512],
                                     op0=ALU.mult, op1=ALU.add), inc="v")
                            release_bank(bk, "v")

        C.wait("sync", "v")
        for i in range(NT):
            C.op("sync", lambda E, i=i: E.dma_start(
                out=out_h[i * 128:(i + 1) * 128, :],
                in_=acc[:, i * H:(i + 1) * H]), inc="sp", by=16)
        C.wait("sync", "sp")

        with nc.Block() as block:
            @block.sync
            def _(E):
                for f in C.q["sync"]:
                    f(E)

            @block.vector
            def _(E):
                for f in C.q["vector"]:
                    f(E)

            @block.scalar
            def _(E):
                for f in C.q["scalar"]:
                    f(E)

            @block.tensor
            def _(E):
                for f in C.q["tensor"]:
                    f(E)

    nc._ctx_trace = C.trace
    return nc


def _fmaj(v):
    return np.ascontiguousarray(v.reshape(NF, 128).T)


def _bcast(s):
    return np.full((128, 1), s, dtype=np.float32)


def make_inputs(x, wp, wdl, bd, A1, B1, D, gw, uw, dw, sin, sind, T):
    common = {
        "wdp": np.ascontiguousarray(wp[0:H, :]),
        "wc": np.ascontiguousarray(wp[2 * H:3 * H, :]),
        "wres": np.ascontiguousarray(wp[3 * H:4 * H, :]),
        "wdl": wdl, "bd": _fmaj(bd), "a1": _fmaj(A1), "b1": _fmaj(B1),
        "dvec": _bcast(D), "sin": _bcast(sin), "sind": _bcast(sind),
        "gw": gw, "uw": uw, "dw": dw,
        "ident": np.eye(128, dtype=np.float32),
        "ones1": np.ones((1, 128), dtype=np.float32),
    }
    B = x.shape[0]
    in_maps = []
    for core in range(2 * B):
        b, half = core // 2, core % 2
        m = dict(common)
        m["x_own"] = np.ascontiguousarray(x[b, half * T:(half + 1) * T, :])
        m["x_oth"] = np.ascontiguousarray(x[b, (1 - half) * T:(2 - half) * T, :])
        m["gate"] = _bcast(1.0 if half == 1 else 0.0)
        in_maps.append(m)
    return in_maps


def kernel(L0_attn_x, L0_attn_W_proj, L0_attn_A, L0_attn_Bp, L0_attn_D,
           L0_attn_W_delta, L0_attn_b_delta, L0_ffn_gate_w, L0_ffn_up_w,
           L0_ffn_down_w, L0_ffn_s_in, L0_ffn_s_in_down):
    B, L, _ = L0_attn_x.shape
    T = L // 2
    x = np.ascontiguousarray(np.asarray(L0_attn_x, dtype=np.float32))
    wp = np.ascontiguousarray(np.asarray(L0_attn_W_proj, dtype=np.float32))
    A1 = np.ascontiguousarray(np.asarray(L0_attn_A, dtype=np.float32)[0, 0])
    B1 = np.ascontiguousarray(np.asarray(L0_attn_Bp, dtype=np.float32)[0, :, 0])
    D = np.asarray(L0_attn_D, dtype=np.float32).reshape(-1)[0]
    wdl = np.ascontiguousarray(np.asarray(L0_attn_W_delta, dtype=np.float32))
    bd = np.ascontiguousarray(np.asarray(L0_attn_b_delta, dtype=np.float32))
    gw = np.ascontiguousarray(np.asarray(L0_ffn_gate_w, dtype=np.float32))
    uw = np.ascontiguousarray(np.asarray(L0_ffn_up_w, dtype=np.float32))
    dw = np.ascontiguousarray(np.asarray(L0_ffn_down_w, dtype=np.float32))
    sin = np.asarray(L0_ffn_s_in, dtype=np.float32).reshape(-1)[0]
    sind = np.asarray(L0_ffn_s_in_down, dtype=np.float32).reshape(-1)[0]

    in_maps = make_inputs(x, wp, wdl, bd, A1, B1, D, gw, uw, dw, sin, sind, T)
    nc = build_program(T, T, dbg=False)
    res = run_bass_kernel_spmd(nc, in_maps, core_ids=list(range(8)))
    out = np.empty((B, L, H), dtype=np.float32)
    for core in range(2 * B):
        b, half = core // 2, core % 2
        out[b, half * T:(half + 1) * T, :] = res.results[core]["out"]
    return out


# version 02 (final)
# speedup vs baseline: 1.0121x; 1.0121x over previous
import numpy as np
from contextlib import ExitStack

import concourse.bass as bass
import concourse.mybir as mybir
from concourse.bass_utils import run_bass_kernel_spmd
import ml_dtypes

F32 = mybir.dt.float32
BF16 = mybir.dt.bfloat16
ALU = mybir.AluOpType
AF = mybir.ActivationFunctionType
AX = mybir.AxisListType

H = 1024
II = 4096
NF = H // 128
NI = II // 128
ENGINES = ("sync", "vector", "scalar", "tensor", "gpsimd")


class Ctx:
    def __init__(self, nc, es):
        self.nc = nc
        self.es = es
        self.q = {e: [] for e in ENGINES}
        self.cnt = {}
        self.sems = {}
        self.trace = {e: [] for e in ENGINES}

    def sem(self, name):
        h = self.es.enter_context(self.nc.semaphore(name))
        self.sems[name] = h
        self.cnt[name] = 0

    def _line(self):
        import sys
        f = sys._getframe(2)
        return f.f_lineno

    def wait(self, eng, sem, val=None):
        v = self.cnt[sem] if val is None else val
        if v <= 0:
            return
        h = self.sems[sem]
        self.trace[eng].append(("w", sem, v, self._line()))
        self.q[eng].append(lambda E, h=h, v=v: E.wait_ge(h, v))

    def op(self, eng, build, inc=None, by=1):
        if inc is None:
            self.trace[eng].append(("o", None, 0, self._line()))
            self.q[eng].append(build)
            return None
        self.cnt[inc] += by
        h = self.sems[inc]
        self.trace[eng].append(("o", inc, self.cnt[inc], self._line()))
        self.q[eng].append(lambda E, b=build, h=h, by=by: b(E).then_inc(h, by))
        return self.cnt[inc]

    def c(self, sem):
        return self.cnt[sem]


def build_program(T, O, dbg=False, upto=None, noreuse=False):
    NT = T // 128
    NO = O // 128
    NS = NT + NO

    nc = bass.Bass("TRN2", target_bir_lowering=False)

    def din(name, shape, dt=F32):
        return nc.dram_tensor(name, shape, dt, kind="ExternalInput").ap()

    def dout(name, shape, dt=F32):
        return nc.dram_tensor(name, shape, dt, kind="ExternalOutput").ap()

    x_own = din("x_own", [T, H])
    x_oth = din("x_oth", [O, H])
    wdp_h = din("wdp", [H, H])
    wc_h = din("wc", [H, H])
    wres_h = din("wres", [H, H])
    wdl_h = din("wdl", [H, H])
    bd_h = din("bd", [128, NF])
    a1_h = din("a1", [128, NF])
    b1_h = din("b1", [128, NF])
    dvec_h = din("dvec", [128, 1])
    sin_h = din("sin", [128, 1])
    sind_h = din("sind", [128, 1])
    gw_h = din("gw", [II, H])
    uw_h = din("uw", [II, H])
    dw_h = din("dw", [H, II])
    ident_h = din("ident", [128, 128])
    ones1_h = din("ones1", [1, 128])
    gate_h = din("gate", [128, 1])

    out_h = dout("out", [T, H])
    dbg_h = {}
    if dbg:
        dbg_h["d_xn"] = dout("d_xn", [T, H])
        dbg_h["d_stats"] = dout("d_stats", [128, 2 * NS])
        dbg_h["d_ucs"] = dout("d_ucs", [128, 2 * NF])
        dbg_h["d_yat"] = dout("d_yat", [T, H])
        dbg_h["d_xq"] = dout("d_xq", [T, H])
        dbg_h["d_hq"] = dout("d_hq", [T, II])
        dbg_h["d_gq"] = dout("d_gq", [II, H])
        dbg_h["d_att"] = dout("d_att", [128, 4 * NF])
        dbg_h["d_parts"] = dout("d_parts", [128, 2 * NF * NS])
        dbg_h["d_cs0"] = dout("d_cs0", [H, 128])
        dbg_h["d_xnto"] = dout("d_xnto", [H, 128])
        dbg_h["d_wct"] = dout("d_wct", [H, 128])

    s_gq = nc.dram_tensor("s_gq", [II, H], BF16).ap()
    s_uq = nc.dram_tensor("s_uq", [II, H], BF16).ap()
    s_dq = nc.dram_tensor("s_dq", [H, II], BF16).ap()
    s_hq = nc.dram_tensor("s_hq", [T, II], BF16).ap()

    es = ExitStack()
    C = Ctx(nc, es)
    with es:
        def sb(name, shape, dt=F32, st=None):
            return (st or es).enter_context(nc.sbuf_tensor(name, shape, dt))

        acc = sb("acc", [128, NT * H])
        xqT = sb("xqT", [128, NF * T], BF16)
        m_all = sb("m_all", [128, NS])
        r_all = sb("r_all", [128, NS])
        uc = sb("uc", [128, NF])
        csum = sb("csum", [128, NF])
        b1uc = sb("b1uc", [128, NF])
        s_oth = sb("s_oth", [128, NF])
        carry = sb("carry", [128, NF])
        bdv = sb("bdv", [128, NF])
        a1v = sb("a1v", [128, NF])
        b1v = sb("b1v", [128, NF])
        dv = sb("dv", [128, 1])
        sinv = sb("sinv", [128, 1])
        sindv = sb("sindv", [128, 1])
        gatev = sb("gatev", [128, 1])
        identv = sb("identv", [128, 128])
        ones1v = sb("ones1v", [1, 128])
        rgs_x = sb("rgs_x", [128, 1])
        rgs_hd = sb("rgs_hd", [128, 1])
        scale_g = sb("scale_g", [128, 1])
        scale_u = sb("scale_u", [128, 1])
        scale_d = sb("scale_d", [128, 1])
        gsw = sb("gsw", [128, 128])
        rgsw = sb("rgsw", [128, 128])
        gmx_run = sb("gmx_run", [128, 8])
        gmx_parts = sb("gmx_parts", [128, 3 * NI])
        gmx1 = sb("gmx1", [4, 1])
        gmx_row = sb("gmx_row", [1, 128])
        q_amax = sb("q_amax", [128, 128])
        q_sc = sb("q_sc", [128, 128])
        q_t0 = sb("q_t0", [128, 128])
        q_step = sb("q_step", [128, 128])
        q_mi = sb("q_mi", [128, 128])
        q_sc8 = sb("q_sc8", [128, 128])
        q_rdq = sb("q_rdq", [128, 128])
        q_hsc = sb("q_hsc", [128, 128], BF16)
        q_pl_ref = [None]
        q_zs = sb("q_zs", [128, 128])
        q_u = sb("q_u", [128, 1024])
        q_z = sb("q_z", [128, 1024])
        q_p1 = sb("q_p1", [128, 1024], BF16)
        q_p2 = sb("q_p2", [128, 1024], BF16)
        q_sg = sb("q_sg", [128, 1024], BF16)
        q_ob = sb("q_ob", [128, 1024], BF16)
        q_ob2 = sb("q_ob2", [128, 1024], BF16)
        zer = sb("zer", [128, 128])
        junk = sb("junkt", [128, 128])
        junk2 = sb("junk2", [128, 128])
        uc_parts = sb("uc_parts", [128, NF * NS])
        cs_parts = sb("cs_parts", [128, NF * NS])
        so_parts = sb("so_parts", [128, NF * NO])
        xst = [sb(f"xst{i}", [128, H]) for i in range(2)]
        xn_t = sb("xn_t", [128, H])
        xnb = [sb(f"xnb{i}", [128, H], BF16) for i in range(2)]
        xnTo = [sb(f"xnTo{i}", [128, NF * 128], BF16) for i in range(2)]

        PS = [es.enter_context(nc.psum_tensor(f"ps{i}", [128, 512], F32))
              for i in range(8)]

        for s in ["ld", "tp", "sp", "v", "s", "t"]:
            C.sem(s)

        bank_free = {i: ("v", 0) for i in range(8)}
        bank_fill = {i: 0 for i in range(8)}

        def mm_group(bank, emit_mms):
            sem, val = bank_free[bank]
            C.wait("tensor", sem, val)
            emit_mms()
            bank_fill[bank] = C.c("t")

        def consume_bank(bank, eng="vector"):
            C.wait(eng, "t", bank_fill[bank])

        def release_bank(bank, sem):
            bank_free[bank] = (sem, C.c(sem))

        def sep():
            C.op("vector", lambda E: E.tensor_scalar_mul(
                out=junk2[:, :], in0=junk[:, :], scalar1=1.0))

        ORDER = ["P2", "P4", "P3", "P5", "F1", "F2"]

        def _ph(p):
            return upto is None or ORDER.index(p) <= ORDER.index(upto)

        for a, t in [(bd_h, bdv), (a1_h, a1v), (b1_h, b1v), (dvec_h, dv),
                     (sin_h, sinv), (sind_h, sindv), (gate_h, gatev),
                     (ident_h, identv), (ones1_h, ones1v)]:
            C.op("sync", lambda E, a=a, t=t: E.dma_start(out=t[:, :], in_=a),
                 inc="ld", by=16)
        for buf, val in [(junk, 1.0), (junk2, 1.0), (zer, 0.0), (gmx_row, 1.0)]:
            C.op("vector", lambda E, b=buf, v=val: E.memset(b[:, :], v))
        for buf in [uc_parts, cs_parts, so_parts, gmx_run]:
            C.op("vector", lambda E, b=buf: E.memset(b[:, :], 0.0))
        C.op("vector", lambda E: E.tensor_tensor_scan(
            out=junk[:, 0:64], data0=junk[:, 0:64], data1=junk[:, 64:128],
            initial=0.0, op0=ALU.add, op1=ALU.add))
        C.op("vector", lambda E: E.tensor_reduce(
            out=junk[:, 0:8], in_=junk[:, 0:128].rearrange("p (n k) -> p n k", k=16),
            axis=AX.X, op=ALU.max, apply_absolute_value=True))
        C.op("vector", lambda E: E.reciprocal(out=junk[:, 0:64], in_=junk[:, 64:128]))
        C.op("vector", lambda E: E.memset(junk[:, :], 1.0), inc="v")
        ld_consts = C.c("ld")

        for i in range(NT):
            C.op("sync", lambda E, i=i: E.dma_start(
                out=acc[:, i * H:(i + 1) * H],
                in_=x_own[i * 128:(i + 1) * 128, :]), inc="ld", by=16)

        def emit_stats(src_ap, col):
            C.op("vector", lambda E, src_ap=src_ap, col=col: E.tensor_reduce(
                out=m_all[:, col:col + 1], in_=src_ap, axis=AX.X, op=ALU.add),
                inc="v")
            C.wait("scalar", "v")
            C.op("scalar", lambda E, src_ap=src_ap: E.activation(
                out=xn_t[:, :], in_=src_ap, func=AF.Square), inc="s")
            C.wait("vector", "s")
            C.op("vector", lambda E, col=col: E.tensor_reduce(
                out=r_all[:, col:col + 1], in_=xn_t[:, :], axis=AX.X, op=ALU.add),
                inc="v")

        for i in range(NT):
            C.wait("vector", "ld", ld_consts + 16 * (i + 1))
            emit_stats(acc[:, i * H:(i + 1) * H], i)
        for i in range(NO):
            b = i % 2
            C.wait("sync", "s")
            C.op("sync", lambda E, i=i, b=b: E.dma_start(
                out=xst[b][:, :], in_=x_oth[i * 128:(i + 1) * 128, :]),
                inc="ld", by=16)
            C.wait("vector", "ld")
            emit_stats(xst[b][:, :], NT + i)

        C.op("vector", lambda E: E.tensor_scalar_mul(
            out=m_all[:, 0:NS], in0=m_all[:, 0:NS], scalar1=float(1.0 / H)))
        C.op("vector", lambda E: E.tensor_scalar_mul(
            out=r_all[:, 0:NS], in0=r_all[:, 0:NS], scalar1=float(1.0 / H)))
        sep()
        C.op("vector", lambda E: E.tensor_tensor(
            out=q_t0[:, 0:NS], in0=m_all[:, 0:NS], in1=m_all[:, 0:NS], op=ALU.mult))
        sep()
        C.op("vector", lambda E: E.tensor_sub(
            out=r_all[:, 0:NS], in0=r_all[:, 0:NS], in1=q_t0[:, 0:NS]))
        sep()
        C.op("vector", lambda E: E.tensor_scalar_add(
            out=r_all[:, 0:NS], in0=r_all[:, 0:NS], scalar1=1e-5), inc="v")
        C.wait("scalar", "v")
        C.op("scalar", lambda E: E.activation(
            out=q_zs[:, 0:NS], in_=r_all[:, 0:NS], func=AF.Sqrt), inc="s")
        C.wait("vector", "s")
        C.op("vector", lambda E: E.reciprocal(out=q_t0[:, 0:NS], in_=q_zs[:, 0:NS]))
        sep()
        C.op("vector", lambda E: E.tensor_tensor(
            out=q_t0[:, 0:NS], in0=r_all[:, 0:NS], in1=q_t0[:, 0:NS], op=ALU.mult))
        sep()
        C.op("vector", lambda E: E.tensor_scalar_mul(
            out=q_t0[:, 0:NS], in0=q_t0[:, 0:NS], scalar1=0.5))
        sep()
        C.op("vector", lambda E: E.scalar_tensor_tensor(
            out=q_t0[:, 0:NS], in0=q_zs[:, 0:NS], scalar=0.5, in1=q_t0[:, 0:NS],
            op0=ALU.mult, op1=ALU.add))
        sep()
        C.op("vector", lambda E: E.reciprocal(out=r_all[:, 0:NS], in_=q_t0[:, 0:NS]))
        sep()
        sep()
        C.op("vector", lambda E: E.tensor_scalar_mul(
            out=junk2[:, :], in0=junk[:, :], scalar1=1.0), inc="v")
        if dbg:
            C.wait("sync", "v")
            C.op("sync", lambda E: E.dma_start(
                out=dbg_h["d_stats"][:, 0:NS], in_=m_all[:, 0:NS]), inc="sp", by=16)
            C.op("sync", lambda E: E.dma_start(
                out=dbg_h["d_stats"][:, NS:2 * NS], in_=r_all[:, 0:NS]),
                inc="sp", by=16)

        def emit_xn_tile(src_ap, col, out_f32, out_bf16):
            if out_f32 is not None:
                C.op("vector", lambda E: E.tensor_scalar(
                    out=out_f32, in0=src_ap,
                    scalar1=m_all[:, col:col + 1], scalar2=r_all[:, col:col + 1],
                    op0=ALU.subtract, op1=ALU.mult))
                C.op("vector", lambda E: E.tensor_copy(out=out_bf16, in_=out_f32),
                     inc="v")
            else:
                C.op("vector", lambda E: E.tensor_scalar(
                    out=out_bf16, in0=src_ap,
                    scalar1=m_all[:, col:col + 1], scalar2=r_all[:, col:col + 1],
                    op0=ALU.subtract, op1=ALU.mult), inc="v")

        def emit_tr8(src_bf16, dst_cols):
            C.wait("scalar", "v")
            for j in range(NF):
                C.op("scalar", lambda E, j=j: E.dma_start_transpose(
                    out=dst_cols(j), in_=src_bf16[:, j * 128:(j + 1) * 128]),
                    inc="tp", by=16)

        def stream_tile(gi):
            b = gi % 2
            if gi < NT:
                C.wait("vector", "tp")
                emit_xn_tile(acc[:, gi * H:(gi + 1) * H], gi, None, xnb[b][:, :])
            else:
                i = gi - NT
                C.wait("sync", "v")
                C.op("sync", lambda E, i=i, b=b: E.dma_start(
                    out=xst[b][:, :], in_=x_oth[i * 128:(i + 1) * 128, :]),
                    inc="ld", by=16)
                C.wait("vector", "ld")
                C.wait("vector", "tp")
                emit_xn_tile(xst[b][:, :], gi, None, xnb[b][:, :])
            C.wait("scalar", "t")
            emit_tr8(xnb[b], lambda j, b=b: xnTo[b][:, j * 128:(j + 1) * 128])
            C.wait("tensor", "tp")
            return b

        esA1 = ExitStack()
        if noreuse:
            es.enter_context(esA1)
        with esA1:
            wcT = sb("wcT", [128, NF * H], BF16, esA1)
            wldA = sb("wld", [128, H], st=esA1)
            wldbA = sb("wldb", [128, H], BF16, st=esA1)

            def emit_wT(w_in, wT):
                for fi in range(NF):
                    C.wait("sync", "v")
                    C.op("sync", lambda E, fi=fi, w_in=w_in, wld=wldA:
                         E.dma_start(
                             out=wld[:, :], in_=w_in[fi * 128:(fi + 1) * 128, :]),
                         inc="ld", by=16)
                    C.wait("vector", "ld")
                    C.wait("vector", "tp")
                    C.op("vector", lambda E, wld=wldA, wldb=wldbA: E.tensor_copy(
                        out=wldb[:, :], in_=wld[:, :]), inc="v")
                    emit_tr8(wldbA, lambda j, fi=fi, wT=wT:
                             wT[:, j * H + fi * 128: j * H + (fi + 1) * 128])

            emit_wT(wc_h, wcT)
            C.wait("tensor", "tp")

            for gi in range(NS):
                b = stream_tile(gi)

                def rhs(j, b=b):
                    return xnTo[b][:, j * 128:(j + 1) * 128]

                for fi in range(NF):
                    bank = fi % 2

                    def mms(fi=fi, bank=bank, rhs=rhs):
                        for j in range(NF):
                            C.op("tensor", lambda E, fi=fi, j=j, bank=bank,
                                 rhs=rhs: E.matmul(
                                     PS[bank][:, 0:128],
                                     wcT[:, j * H + fi * 128:
                                         j * H + (fi + 1) * 128],
                                     rhs(j),
                                     start=(j == 0), stop=(j == NF - 1)),
                                 inc=("t" if j == NF - 1 else None))
                    mm_group(bank, mms)
                    consume_bank(bank)
                    if dbg and gi == 0:
                        C.op("vector", lambda E, bank=bank: E.tensor_copy(
                            out=q_zs[:, :], in_=PS[bank][:, 0:128]), inc="v")
                        C.wait("sync", "v")
                        C.op("sync", lambda E, fi=fi: E.dma_start(
                            out=dbg_h["d_cs0"][fi * 128:(fi + 1) * 128, :],
                            in_=q_zs[:, :]), inc="sp", by=16)
                        C.wait("vector", "sp")
                        if fi == 0:
                            for j in range(NF):
                                C.op("vector", lambda E, j=j, rhs=rhs:
                                     E.tensor_copy(out=q_zs[:, :],
                                                   in_=rhs(j)), inc="v")
                                C.wait("sync", "v")
                                C.op("sync", lambda E, j=j: E.dma_start(
                                    out=dbg_h["d_xnto"][j * 128:(j + 1) * 128, :],
                                    in_=q_zs[:, :]), inc="sp", by=16)
                                C.wait("vector", "sp")
                                C.op("vector", lambda E, j=j: E.tensor_copy(
                                    out=q_zs[:, :],
                                    in_=wcT[:, j * H:j * H + 128]), inc="v")
                                C.wait("sync", "v")
                                C.op("sync", lambda E, j=j: E.dma_start(
                                    out=dbg_h["d_wct"][j * 128:(j + 1) * 128, :],
                                    in_=q_zs[:, :]), inc="sp", by=16)
                                C.wait("vector", "sp")
                    C.op("vector", lambda E, fi=fi, bank=bank, gi=gi, rhs=rhs:
                         E.scalar_tensor_tensor(
                             out=q_z[:, 0:128], in0=rhs(fi), scalar=1.0,
                             in1=PS[bank][:, 0:128], op0=ALU.mult, op1=ALU.mult,
                             accum_out=uc_parts[:, fi * NS + gi:
                                                fi * NS + gi + 1]))
                    C.op("vector", lambda E, fi=fi, bank=bank, gi=gi:
                         E.tensor_reduce(
                             out=cs_parts[:, fi * NS + gi:fi * NS + gi + 1],
                             in_=PS[bank][:, 0:128], axis=AX.X, op=ALU.add),
                         inc="v")
                    release_bank(bank, "v")

        if dbg:
            C.wait("sync", "v")
            C.op("sync", lambda E: E.dma_start(
                out=dbg_h["d_parts"][:, 0:NF * NS], in_=uc_parts[:, :]),
                inc="sp", by=16)
            C.op("sync", lambda E: E.dma_start(
                out=dbg_h["d_parts"][:, NF * NS:], in_=cs_parts[:, :]),
                inc="sp", by=16)
            C.wait("vector", "sp")
        for fi in range(NF):
            C.op("vector", lambda E, fi=fi: E.tensor_reduce(
                out=uc[:, fi:fi + 1], in_=uc_parts[:, fi * NS:(fi + 1) * NS],
                axis=AX.X, op=ALU.add))
            C.op("vector", lambda E, fi=fi: E.tensor_reduce(
                out=csum[:, fi:fi + 1], in_=cs_parts[:, fi * NS:(fi + 1) * NS],
                axis=AX.X, op=ALU.add))
        sep()
        sep()
        sep()
        C.op("vector", lambda E: E.tensor_tensor(
            out=b1uc[:, :], in0=b1v[:, :], in1=uc[:, :], op=ALU.mult), inc="v")
        if dbg:
            C.wait("sync", "v")
            C.op("sync", lambda E: E.dma_start(
                out=dbg_h["d_ucs"][:, 0:NF], in_=uc[:, :]), inc="sp", by=16)
            C.op("sync", lambda E: E.dma_start(
                out=dbg_h["d_ucs"][:, NF:2 * NF], in_=csum[:, :]), inc="sp", by=16)

        def emit_block_stage(nb, gs_ap, rgs_ap):
            pl = q_pl_ref[0]
            sep()
            C.op("vector", lambda E, gs_ap=gs_ap, nb=nb: E.tensor_scalar(
                out=q_sc[:, 0:nb], in0=q_amax[:, 0:nb], scalar1=gs_ap,
                scalar2=2688.0, op0=ALU.mult, op1=ALU.min))
            sep()
            for k in range(-5, 9):
                i = k + 5
                C.op("vector", lambda E, k=k, i=i, nb=nb, pl=pl: E.tensor_scalar(
                    out=pl[:, i * nb:(i + 1) * nb], in0=q_sc[:, 0:nb],
                    scalar1=float(6.0 * 2.0 ** k), scalar2=1.0,
                    op0=ALU.is_ge, op1=ALU.add))
            sep()
            C.op("vector", lambda E, nb=nb, pl=pl: E.tensor_reduce(
                out=q_step[:, 0:nb],
                in_=pl[:, 0:14 * nb].rearrange("p (k n) -> p n k", n=nb),
                axis=AX.X, op=ALU.mult))
            sep()
            C.op("vector", lambda E, nb=nb: E.tensor_scalar_mul(
                out=q_step[:, 0:nb], in0=q_step[:, 0:nb],
                scalar1=float(2.0 ** -9)))
            sep()
            C.op("vector", lambda E, nb=nb: E.reciprocal(
                out=q_t0[:, 0:nb], in_=q_step[:, 0:nb]))
            sep()
            sep()
            C.op("vector", lambda E, nb=nb: E.tensor_tensor(
                out=q_t0[:, 0:nb], in0=q_sc[:, 0:nb], in1=q_t0[:, 0:nb],
                op=ALU.mult))
            sep()
            C.op("vector", lambda E, nb=nb: E.tensor_single_scalar(
                out=q_mi[:, 0:nb], in_=q_t0[:, 0:nb], scalar=3.0, op=ALU.is_ge))
            for j in range(1, 16):
                C.op("vector", lambda E, j=j, nb=nb: E.scalar_tensor_tensor(
                    out=q_mi[:, 0:nb], in0=q_t0[:, 0:nb],
                    scalar=float(3 * (2 * j + 1)),
                    in1=q_mi[:, 0:nb], op0=ALU.is_ge, op1=ALU.add))
            sep()
            C.op("vector", lambda E, nb=nb: E.tensor_tensor(
                out=q_sc8[:, 0:nb], in0=q_mi[:, 0:nb], in1=q_step[:, 0:nb],
                op=ALU.mult))
            sep()
            C.op("vector", lambda E, rgs_ap=rgs_ap, nb=nb: E.tensor_single_scalar(
                out=q_t0[:, 0:nb], in_=q_sc8[:, 0:nb], scalar=rgs_ap,
                op=ALU.mult))
            sep()
            C.op("vector", lambda E, nb=nb: E.scalar_tensor_tensor(
                out=q_t0[:, 0:nb], in0=q_t0[:, 0:nb], scalar=0.0,
                in1=q_t0[:, 0:nb], op0=ALU.is_equal, op1=ALU.add))
            sep()
            C.op("vector", lambda E, nb=nb: E.reciprocal(
                out=q_rdq[:, 0:nb], in_=q_t0[:, 0:nb]))
            C.op("vector", lambda E, nb=nb: E.tensor_scalar_mul(
                out=q_hsc[:, 0:nb], in0=q_sc8[:, 0:nb], scalar1=0.5))
            sep()
            sep()
            C.op("vector", lambda E: E.tensor_scalar_mul(
                out=junk2[:, :], in0=junk[:, :], scalar1=1.0), inc="v")

        def emit_elem_stage(src_f32, nb, out_bf16, amax_off=0):
            F = 16 * nb
            rdq_b = q_rdq[:, amax_off:amax_off + nb].unsqueeze(2).broadcast_to(
                [128, nb, 16])
            hsc_b = q_hsc[:, amax_off:amax_off + nb].unsqueeze(2).broadcast_to(
                [128, nb, 16])

            def v3(ap):
                return ap.rearrange("p (n k) -> p n k", k=16)

            C.op("vector", lambda E: E.tensor_tensor(
                out=v3(q_u[:, 0:F]), in0=v3(src_f32), in1=rdq_b, op=ALU.mult),
                inc="v")
            C.wait("scalar", "v")
            C.op("scalar", lambda E: E.activation(
                out=q_z[:, 0:F], in_=q_u[:, 0:F], func=AF.Square))
            C.op("scalar", lambda E: E.activation(
                out=q_sg[:, 0:F], in_=q_u[:, 0:F], func=AF.Sign), inc="s")
            C.wait("vector", "s")
            C.op("vector", lambda E: E.tensor_single_scalar(
                out=q_p1[:, 0:F], in_=q_z[:, 0:F], scalar=0.0625, op=ALU.is_ge))
            for thr in (0.5625, 1.5625, 3.0625):
                C.op("vector", lambda E, thr=thr: E.scalar_tensor_tensor(
                    out=q_p1[:, 0:F], in0=q_z[:, 0:F], scalar=thr,
                    in1=q_p1[:, 0:F], op0=ALU.is_ge, op1=ALU.add))
            C.op("vector", lambda E: E.tensor_single_scalar(
                out=q_p2[:, 0:F], in_=q_z[:, 0:F], scalar=6.25, op=ALU.is_ge))
            for thr in (12.25, 25.0, 25.0):
                C.op("vector", lambda E, thr=thr: E.scalar_tensor_tensor(
                    out=q_p2[:, 0:F], in0=q_z[:, 0:F], scalar=thr,
                    in1=q_p2[:, 0:F], op0=ALU.is_ge, op1=ALU.add))
            C.op("vector", lambda E: E.scalar_tensor_tensor(
                out=q_p1[:, 0:F], in0=q_p2[:, 0:F], scalar=2.0, in1=q_p1[:, 0:F],
                op0=ALU.mult, op1=ALU.add))
            C.op("vector", lambda E: E.tensor_mul(
                out=q_p1[:, 0:F], in0=q_p1[:, 0:F], in1=q_sg[:, 0:F]))
            C.op("vector", lambda E: E.tensor_tensor(
                out=v3(out_bf16), in0=v3(q_p1[:, 0:F]), in1=hsc_b, op=ALU.mult),
                inc="v")

        def emit_amax(src_f32, nb, off=0):
            C.op("vector", lambda E: E.tensor_reduce(
                out=q_amax[:, off:off + nb],
                in_=src_f32.rearrange("p (n k) -> p n k", k=16),
                axis=AX.X, op=ALU.max, apply_absolute_value=True), inc="v")

        if _ph("P4"):
            esP4 = ExitStack()
            q_pl_ref[0] = sb("q_pl4", [128, 14 * 64], st=esP4)
            C.op("vector", lambda E: E.reciprocal(out=rgs_x[:, :], in_=sinv[:, :]))
            C.op("vector", lambda E: E.reciprocal(out=rgs_hd[:, :], in_=sindv[:, :]))
            sep()
            sep()
            for i in range(NT):
                C.wait("vector", "tp")
                emit_xn_tile(acc[:, i * H:(i + 1) * H], i, xn_t[:, :], xnb[0][:, :])
                emit_amax(xn_t[:, :], 64)
                emit_block_stage(64, sinv[:, 0:1], rgs_x[:, 0:1])
                emit_elem_stage(xn_t[:, :], 64, q_ob[:, :])
                emit_tr8(q_ob, lambda j, i=i: xqT[:, j * T + i * 128:
                                                 j * T + (i + 1) * 128])
                if dbg:
                    C.op("vector", lambda E: E.tensor_copy(
                        out=q_z[:, :], in_=q_ob[:, :]), inc="v")
                    C.wait("sync", "v")
                    C.op("sync", lambda E, i=i: E.dma_start(
                        out=dbg_h["d_xq"][i * 128:(i + 1) * 128, :],
                        in_=q_z[:, :]), inc="sp", by=16)
                    C.wait("vector", "sp")
            if dbg:
                for i in range(NT):
                    emit_xn_tile(acc[:, i * H:(i + 1) * H], i, xn_t[:, :],
                                 xnb[0][:, :])
                    C.wait("sync", "v")
                    C.op("sync", lambda E, i=i: E.dma_start(
                        out=dbg_h["d_xn"][i * 128:(i + 1) * 128, :],
                        in_=xn_t[:, :]), inc="sp", by=16)
                    C.wait("vector", "sp")
            esP4.close()
            q_pl_ref[0] = None

        if _ph("P3"):
            esA2 = ExitStack()
            with esA2:
                wdpT = sb("wdpT", [128, NF * H], BF16, esA2)
                wdlT = sb("wdlT", [128, NF * H], BF16, esA2)
                wresT = sb("wresT", [128, NF * H], BF16, esA2)
                wldA2 = sb("wld2w", [128, H], st=esA2)
                wldbA2 = sb("wldb2", [128, H], BF16, st=esA2)
                dinT = sb("dinT", [128, NF * 128], BF16, st=esA2)
                at_d = sb("at_d", [128, 128], st=esA2)
                at_e = sb("at_e", [128, 128], st=esA2)
                at_s = sb("at_s", [128, 128], st=esA2)
                at_c = [sb(f"at_c{i}", [128, 128], st=esA2) for i in range(2)]
                at_y = sb("at_y", [128, 128], st=esA2)

                def emit_wT2(w_in, wT):
                    for fi in range(NF):
                        C.wait("sync", "v")
                        C.op("sync", lambda E, fi=fi, w_in=w_in, wld=wldA2:
                             E.dma_start(
                                 out=wld[:, :],
                                 in_=w_in[fi * 128:(fi + 1) * 128, :]),
                             inc="ld", by=16)
                        C.wait("vector", "ld")
                        C.wait("vector", "tp")
                        C.op("vector", lambda E, wld=wldA2, wldb=wldbA2:
                             E.tensor_copy(
                                 out=wldb[:, :], in_=wld[:, :]), inc="v")
                        emit_tr8(wldbA2, lambda j, fi=fi, wT=wT:
                                 wT[:, j * H + fi * 128: j * H + (fi + 1) * 128])

                emit_wT2(wdp_h, wdpT)
                emit_wT2(wdl_h, wdlT)
                emit_wT2(wres_h, wresT)
                C.wait("tensor", "tp")

                def delta_for_tile(b, out_cb):
                    C.wait("vector", "t")
                    for fi in range(NF):
                        bank = fi % 2

                        def mms(fi=fi, bank=bank, b=b):
                            for j in range(NF):
                                C.op("tensor", lambda E, fi=fi, j=j, bank=bank, b=b:
                                     E.matmul(PS[bank][:, 0:128],
                                              wdpT[:, j * H + fi * 128:
                                                   j * H + (fi + 1) * 128],
                                              xnTo[b][:, j * 128:(j + 1) * 128],
                                              start=(j == 0), stop=(j == NF - 1)),
                                     inc=("t" if j == NF - 1 else None))
                        mm_group(bank, mms)
                        consume_bank(bank)
                        C.op("vector", lambda E, fi=fi, bank=bank: E.tensor_copy(
                            out=dinT[:, fi * 128:(fi + 1) * 128],
                            in_=PS[bank][:, 0:128]), inc="v")
                        release_bank(bank, "v")
                    C.wait("tensor", "v")
                    for fi in range(NF):
                        bank = 2 + (fi % 2)

                        def mms2(fi=fi, bank=bank):
                            for j in range(NF):
                                C.op("tensor", lambda E, fi=fi, j=j, bank=bank:
                                     E.matmul(PS[bank][:, 0:128],
                                              wdlT[:, j * H + fi * 128:
                                                   j * H + (fi + 1) * 128],
                                              dinT[:, j * 128:(j + 1) * 128],
                                              start=(j == 0), stop=(j == NF - 1)),
                                     inc=("t" if j == NF - 1 else None))
                        mm_group(bank, mms2)
                        consume_bank(bank, "scalar")
                        C.op("scalar", lambda E, fi=fi, bank=bank: E.activation(
                            out=at_e[:, :], in_=PS[bank][:, 0:128], func=AF.Exp,
                            bias=bdv[:, fi:fi + 1]), inc="s")
                        C.op("scalar", lambda E: E.activation(
                            out=at_d[:, :], in_=at_e[:, :], func=AF.Ln,
                            bias=1.0), inc="s")
                        release_bank(bank, "s")
                        out_cb(fi)

                for i in range(NO):
                    b = stream_tile(NT + i)

                    def oth_cb(fi, i=i):
                        C.wait("vector", "s")
                        C.op("vector", lambda E, fi=fi: E.tensor_single_scalar(
                            out=at_s[:, :], in_=at_d[:, :],
                            scalar=b1v[:, fi:fi + 1], op=ALU.mult))
                        C.op("vector", lambda E, fi=fi, i=i: E.tensor_reduce(
                            out=so_parts[:, fi * NO + i:fi * NO + i + 1],
                            in_=at_s[:, :], axis=AX.X, op=ALU.add), inc="v")

                    delta_for_tile(b, oth_cb)

                C.wait("vector", "s")
                for fi in range(NF):
                    C.op("vector", lambda E, fi=fi: E.tensor_reduce(
                        out=s_oth[:, fi:fi + 1],
                        in_=so_parts[:, fi * NO:(fi + 1) * NO],
                        axis=AX.X, op=ALU.add))
                sep()
                sep()
                sep()
                C.op("vector", lambda E: E.tensor_single_scalar(
                    out=carry[:, :], in_=s_oth[:, :], scalar=gatev[:, 0:1],
                    op=ALU.mult))
                sep()
                sep()
                C.op("vector", lambda E: E.tensor_scalar_mul(
                    out=junk2[:, :], in0=junk[:, :], scalar1=1.0), inc="v")
                if dbg:
                    C.wait("sync", "v")
                    C.op("sync", lambda E: E.dma_start(
                        out=dbg_h["d_att"][:, 0:NF], in_=s_oth[:, :]), inc="sp", by=16)
                    C.op("sync", lambda E: E.dma_start(
                        out=dbg_h["d_att"][:, NF:2 * NF], in_=carry[:, :]),
                        inc="sp", by=16)

                for i in range(NT):
                    b = stream_tile(i)
                    cb = i % 2

                    def own_cb(fi, i=i, cb=cb, b=b):
                        C.wait("vector", "s")
                        C.op("vector", lambda E, fi=fi: E.tensor_single_scalar(
                            out=at_s[:, :], in_=at_d[:, :],
                            scalar=b1v[:, fi:fi + 1], op=ALU.mult))
                        C.op("vector", lambda E, fi=fi, cb=cb: E.tensor_tensor_scan(
                            out=at_c[cb][:, :], data0=at_s[:, :], data1=zer[:, :],
                            initial=carry[:, fi:fi + 1], op0=ALU.add, op1=ALU.add))
                        C.wait("scalar", "v", C.c("v"))
                        C.op("scalar", lambda E, fi=fi: E.activation(
                            out=at_e[:, :], in_=at_d[:, :], func=AF.Exp,
                            scale=a1v[:, fi:fi + 1]), inc="s")
                        C.wait("vector", "s")
                        C.op("vector", lambda E, fi=fi: E.tensor_single_scalar(
                            out=at_y[:, :], in_=at_e[:, :],
                            scalar=b1uc[:, fi:fi + 1], op=ALU.mult))
                        C.op("vector", lambda E, fi=fi, cb=cb: E.scalar_tensor_tensor(
                            out=at_y[:, :], in0=at_c[cb][:, :],
                            scalar=csum[:, fi:fi + 1], in1=at_y[:, :],
                            op0=ALU.mult, op1=ALU.add), inc="v")
                        C.op("vector", lambda E, fi=fi, cb=cb: E.tensor_copy(
                            out=carry[:, fi:fi + 1], in_=at_c[cb][:, 127:128]))
                        rbank = 4 + (fi % 2)

                        def mmsr(fi=fi, rbank=rbank, b=b):
                            for j in range(NF):
                                C.op("tensor", lambda E, fi=fi, j=j, rbank=rbank, b=b:
                                     E.matmul(PS[rbank][:, 0:128],
                                              wresT[:, j * H + fi * 128:
                                                    j * H + (fi + 1) * 128],
                                              xnTo[b][:, j * 128:(j + 1) * 128],
                                              start=(j == 0), stop=(j == NF - 1)),
                                     inc=("t" if j == NF - 1 else None))
                        mm_group(rbank, mmsr)
                        consume_bank(rbank)
                        C.op("vector", lambda E, rbank=rbank: E.scalar_tensor_tensor(
                            out=at_y[:, :], in0=PS[rbank][:, 0:128],
                            scalar=dv[:, 0:1], in1=at_y[:, :],
                            op0=ALU.mult, op1=ALU.add), inc="v")
                        release_bank(rbank, "v")
                        tbank = 6 + (fi % 2)
                        sem, val = bank_free[tbank]
                        C.wait("tensor", sem, val)
                        C.wait("tensor", "v", C.c("v"))
                        C.op("tensor", lambda E, tbank=tbank: E.transpose(
                            PS[tbank][:, 0:128], at_y[:, :], identv[:, :]), inc="t")
                        bank_fill[tbank] = C.c("t")
                        consume_bank(tbank)
                        C.op("vector", lambda E, i=i, fi=fi, tbank=tbank: E.tensor_add(
                            out=acc[:, i * H + fi * 128:i * H + (fi + 1) * 128],
                            in0=acc[:, i * H + fi * 128:i * H + (fi + 1) * 128],
                            in1=PS[tbank][:, 0:128]), inc="v")
                        release_bank(tbank, "v")

                    delta_for_tile(b, own_cb)

            if dbg:
                C.wait("sync", "v")
                for i in range(NT):
                    C.op("sync", lambda E, i=i: E.dma_start(
                        out=dbg_h["d_yat"][i * 128:(i + 1) * 128, :],
                        in_=acc[:, i * H:(i + 1) * H]), inc="sp", by=16)

        if _ph("P5"):
            esB = ExitStack()
            with esB:
                wld2 = sb("wld2", [128, II], st=esB)
                q_pl_ref[0] = sb("q_pl5", [128, 14 * 64], st=esB)

                def gmax_pass(w_in, ntiles, wcols, wi):
                    for ti in range(ntiles):
                        C.wait("sync", "v")
                        C.op("sync", lambda E, ti=ti, w_in=w_in, wcols=wcols:
                             E.dma_start(out=wld2[:, 0:wcols],
                                         in_=w_in[ti * 128:(ti + 1) * 128, :]),
                             inc="ld", by=16)
                        C.wait("vector", "ld")
                        C.op("vector", lambda E, ti=ti, wi=wi, wcols=wcols:
                             E.tensor_reduce(
                                 out=gmx_parts[:, wi * NI + ti:wi * NI + ti + 1],
                                 in_=wld2[:, 0:wcols], axis=AX.X,
                                 op=ALU.max, apply_absolute_value=True), inc="v")

                gmax_pass(gw_h, NI, H, 0)
                gmax_pass(uw_h, NI, H, 1)
                gmax_pass(dw_h, NF, II, 2)
                sep()
                sep()
                sep()
                for wi in range(3):
                    C.op("vector", lambda E, wi=wi: E.tensor_reduce(
                        out=gmx_run[:, wi:wi + 1],
                        in_=gmx_parts[:, wi * NI:wi * NI + (NI if wi < 2 else NF)],
                        axis=AX.X, op=ALU.max))
                sep()
                sep()
                C.op("vector", lambda E: E.tensor_scalar_mul(
                    out=junk2[:, :], in0=junk[:, :], scalar1=1.0), inc="v")

                sem, val = bank_free[0]
                C.wait("tensor", sem, val)
                C.wait("tensor", "v", C.c("v"))
                C.op("tensor", lambda E: E.transpose(
                    PS[0][0:3, 0:128], gmx_run[:, 0:3], identv[:, :]), inc="t")
                bank_fill[0] = C.c("t")
                consume_bank(0)
                C.op("vector", lambda E: E.tensor_reduce(
                    out=gmx1[0:3, 0:1], in_=PS[0][0:3, 0:128], axis=AX.X, op=ALU.max),
                    inc="v")
                release_bank(0, "v")
                sem, val = bank_free[1]
                C.wait("tensor", sem, val)
                C.wait("tensor", "v", C.c("v"))
                C.op("tensor", lambda E: E.transpose(
                    PS[1][0:1, 0:3], gmx1[0:3, 0:1], identv[0:3, 0:3]), inc="t")
                bank_fill[1] = C.c("t")
                consume_bank(1)
                C.op("vector", lambda E: E.tensor_copy(
                    out=gmx_row[0:1, 0:3], in_=PS[1][0:1, 0:3]), inc="v")
                release_bank(1, "v")
                sem, val = bank_free[2]
                C.wait("tensor", sem, val)
                C.wait("tensor", "v", C.c("v"))
                C.op("tensor", lambda E: E.matmul(
                    PS[2][:, 0:128], ones1v[0:1, :], gmx_row[0:1, 0:128],
                    start=True, stop=True), inc="t")
                bank_fill[2] = C.c("t")
                consume_bank(2)
                C.op("vector", lambda E: E.tensor_scalar_mul(
                    out=q_zs[:, :], in0=PS[2][:, 0:128], scalar1=1.0))
                sep()
                C.op("vector", lambda E: E.reciprocal(
                    out=gsw[:, :], in_=q_zs[:, :]))
                sep()
                sep()
                C.op("vector", lambda E: E.tensor_scalar_mul(
                    out=gsw[:, :], in0=gsw[:, :], scalar1=2688.0))
                sep()
                C.op("vector", lambda E: E.reciprocal(
                    out=rgsw[:, :], in_=gsw[:, :]))
                sep()
                sep()
                C.op("vector", lambda E: E.tensor_tensor(
                    out=scale_g[:, :], in0=rgs_x[:, :], in1=rgsw[:, 0:1], op=ALU.mult))
                C.op("vector", lambda E: E.tensor_tensor(
                    out=scale_u[:, :], in0=rgs_x[:, :], in1=rgsw[:, 1:2], op=ALU.mult))
                C.op("vector", lambda E: E.tensor_tensor(
                    out=scale_d[:, :], in0=rgs_hd[:, :], in1=rgsw[:, 2:3], op=ALU.mult),
                    inc="v")
                release_bank(2, "v")

                q_obs = [q_ob, q_ob2]
                ob_last = [0, 0]
                ob_idx = [0]

                def wquant_pass(w_in, s_out, ntiles, wcols, wi):
                    nsub = wcols // 1024
                    for ti in range(ntiles):
                        C.wait("sync", "v")
                        C.op("sync", lambda E, ti=ti, w_in=w_in, wcols=wcols:
                             E.dma_start(out=wld2[:, 0:wcols],
                                         in_=w_in[ti * 128:(ti + 1) * 128, :]),
                             inc="ld", by=16)
                        C.wait("vector", "ld")
                        for su in range(nsub):
                            src = wld2[:, su * 1024:(su + 1) * 1024]
                            k = ob_idx[0] % 2
                            ob_idx[0] += 1
                            ob = q_obs[k]
                            emit_amax(src, 64)
                            emit_block_stage(64, gsw[:, wi:wi + 1],
                                             rgsw[:, wi:wi + 1])
                            C.wait("vector", "sp", ob_last[k])
                            emit_elem_stage(src, 64, ob[:, :])
                            C.wait("sync", "v")
                            C.op("sync", lambda E, ti=ti, su=su, s_out=s_out,
                                 ob=ob: E.dma_start(
                                     out=s_out[ti * 128:(ti + 1) * 128,
                                               su * 1024:(su + 1) * 1024],
                                     in_=ob[:, :]), inc="sp", by=16)
                            ob_last[k] = C.c("sp")
                            if dbg and s_out is s_gq:
                                C.wait("vector", "sp")
                                C.op("vector", lambda E, ob=ob: E.tensor_copy(
                                    out=q_z[:, :], in_=ob[:, :]), inc="v")
                                C.wait("sync", "v")
                                C.op("sync", lambda E, ti=ti: E.dma_start(
                                    out=dbg_h["d_gq"][ti * 128:(ti + 1) * 128, :],
                                    in_=q_z[:, :]), inc="sp", by=16)
                                C.wait("vector", "sp")

                wquant_pass(gw_h, s_gq, NI, H, 0)
                wquant_pass(uw_h, s_uq, NI, H, 1)
                wquant_pass(dw_h, s_dq, NF, II, 2)
                C.wait("vector", "sp")

        if _ph("F1"):
            esC = ExitStack()
            with esC:
                q_pl_ref[0] = sb("q_plF", [128, 14 * 128], st=esC)
                wgq = [sb(f"wgq{i}", [128, NF * 512], BF16, esC) for i in range(2)]
                wuq = [sb(f"wuq{i}", [128, NF * 512], BF16, esC) for i in range(2)]
                hid4 = sb("hid4", [128, 2048], st=esC)
                sgt = sb("sgt", [128, 512], st=esC)
                wbuf_free_t = [0, 0]

                for ic in range(II // 512):
                    wb = ic % 2
                    C.wait("scalar", "t", wbuf_free_t[wb])
                    for j in range(NF):
                        C.op("scalar", lambda E, ic=ic, j=j, wb=wb:
                             E.dma_start_transpose(
                                 out=wgq[wb][:, j * 512:(j + 1) * 512],
                                 in_=s_gq[ic * 512:(ic + 1) * 512,
                                          j * 128:(j + 1) * 128]), inc="tp", by=16)
                        C.op("scalar", lambda E, ic=ic, j=j, wb=wb:
                             E.dma_start_transpose(
                                 out=wuq[wb][:, j * 512:(j + 1) * 512],
                                 in_=s_uq[ic * 512:(ic + 1) * 512,
                                          j * 128:(j + 1) * 128]), inc="tp", by=16)
                    C.wait("tensor", "tp")
                    for ttg in range(NT // 4):
                        for k in range(4):
                            tt = ttg * 4 + k
                            gb, ub = 2 * k, 2 * k + 1
                            semg, valg = bank_free[gb]
                            C.wait("tensor", semg, valg)
                            semu, valu = bank_free[ub]
                            C.wait("tensor", semu, valu)
                            for j in range(NF):
                                C.op("tensor", lambda E, tt=tt, j=j, gb=gb, wb=wb:
                                     E.matmul(PS[gb][:, :],
                                              xqT[:, j * T + tt * 128:
                                                  j * T + (tt + 1) * 128],
                                              wgq[wb][:, j * 512:(j + 1) * 512],
                                              start=(j == 0), stop=(j == NF - 1)),
                                     inc=("t" if j == NF - 1 else None))
                            for j in range(NF):
                                C.op("tensor", lambda E, tt=tt, j=j, ub=ub, wb=wb:
                                     E.matmul(PS[ub][:, :],
                                              xqT[:, j * T + tt * 128:
                                                  j * T + (tt + 1) * 128],
                                              wuq[wb][:, j * 512:(j + 1) * 512],
                                              start=(j == 0), stop=(j == NF - 1)),
                                     inc=("t" if j == NF - 1 else None))
                            bank_fill[gb] = bank_fill[ub] = C.c("t")
                        for k in range(4):
                            gb, ub = 2 * k, 2 * k + 1
                            consume_bank(gb, "scalar")
                            C.wait("scalar", "v")
                            C.op("scalar", lambda E, gb=gb: E.activation(
                                out=sgt[:, :], in_=PS[gb][:, :], func=AF.Silu,
                                scale=scale_g[:, 0:1]), inc="s")
                            C.wait("vector", "s")
                            consume_bank(ub)
                            C.op("vector", lambda E, ub=ub, k=k: E.scalar_tensor_tensor(
                                out=hid4[:, k * 512:(k + 1) * 512],
                                in0=PS[ub][:, :], scalar=scale_u[:, 0:1],
                                in1=sgt[:, :], op0=ALU.mult, op1=ALU.mult), inc="v")
                            release_bank(gb, "v")
                            release_bank(ub, "v")
                        emit_amax(hid4[:, :], 128)
                        emit_block_stage(128, sindv[:, 0:1], rgs_hd[:, 0:1])
                        for half in range(2):
                            emit_elem_stage(hid4[:, half * 1024:(half + 1) * 1024],
                                            64, q_ob[:, :], amax_off=half * 64)
                            C.wait("sync", "v")
                            for k2 in range(2):
                                tt = ttg * 4 + half * 2 + k2
                                C.op("sync", lambda E, ic=ic, tt=tt, k2=k2:
                                     E.dma_start(
                                         out=s_hq[tt * 128:(tt + 1) * 128,
                                                  ic * 512:(ic + 1) * 512],
                                         in_=q_ob[:, k2 * 512:(k2 + 1) * 512]),
                                     inc="sp", by=16)
                            C.wait("vector", "sp")
                            if dbg:
                                C.op("vector", lambda E: E.tensor_copy(
                                    out=q_z[:, :], in_=q_ob[:, :]), inc="v")
                                C.wait("sync", "v")
                                for k2 in range(2):
                                    tt = ttg * 4 + half * 2 + k2
                                    C.op("sync", lambda E, ic=ic, tt=tt, k2=k2:
                                         E.dma_start(
                                             out=dbg_h["d_hq"][
                                                 tt * 128:(tt + 1) * 128,
                                                 ic * 512:(ic + 1) * 512],
                                             in_=q_z[:, k2 * 512:(k2 + 1) * 512]),
                                         inc="sp", by=16)
                                C.wait("vector", "sp")
                    wbuf_free_t[wb] = C.c("t")

        if _ph("F2"):
            esD = ExitStack()
            with esD:
                NH = NI // 2
                hqT = [sb(f"hqT{i}", [128, NH * 512], BF16, esD) for i in range(2)]
                wdqT = [sb(f"wdqT{i}", [128, H], BF16, esD) for i in range(2)]
                hbuf_free_t = [0, 0]
                wdq_free_t = [0, 0]
                for c in range(T // 512):
                    for half in range(2):
                        hb = (c * 2 + half) % 2
                        C.wait("scalar", "t", hbuf_free_t[hb])
                        for rr in range(NH):
                            r = half * NH + rr
                            C.op("scalar", lambda E, c=c, r=r, rr=rr, hb=hb:
                                 E.dma_start_transpose(
                                     out=hqT[hb][:, rr * 512:(rr + 1) * 512],
                                     in_=s_hq[c * 512:(c + 1) * 512,
                                              r * 128:(r + 1) * 128]),
                                 inc="tp", by=16)
                        C.wait("tensor", "tp")
                        if half == 0:
                            for bk in range(8):
                                sem, val = bank_free[bk]
                                C.wait("tensor", sem, val)
                        for rr in range(NH):
                            r = half * NH + rr
                            dwb = r % 2
                            C.wait("scalar", "t", wdq_free_t[dwb])
                            C.op("scalar", lambda E, r=r, dwb=dwb:
                                 E.dma_start_transpose(
                                     out=wdqT[dwb][:, 0:512],
                                     in_=s_dq[0:512, r * 128:(r + 1) * 128]),
                                 inc="tp", by=16)
                            C.op("scalar", lambda E, r=r, dwb=dwb:
                                 E.dma_start_transpose(
                                     out=wdqT[dwb][:, 512:1024],
                                     in_=s_dq[512:1024, r * 128:(r + 1) * 128]),
                                 inc="tp", by=16)
                            C.wait("tensor", "tp")
                            for tt in range(4):
                                for hh in range(2):
                                    C.op("tensor", lambda E, r=r, rr=rr, tt=tt,
                                         hh=hh, hb=hb, dwb=dwb: E.matmul(
                                             PS[tt * 2 + hh][:, :],
                                             hqT[hb][:, rr * 512 + tt * 128:
                                                     rr * 512 + (tt + 1) * 128],
                                             wdqT[dwb][:, hh * 512:(hh + 1) * 512],
                                             start=(r == 0), stop=(r == NI - 1)),
                                         inc=("t" if (hh == 1 and tt == 3)
                                              else None))
                            wdq_free_t[dwb] = C.c("t")
                        hbuf_free_t[hb] = C.c("t")
                    for bk in range(8):
                        bank_fill[bk] = C.c("t")
                    for tt in range(4):
                        for hh in range(2):
                            bk = tt * 2 + hh
                            consume_bank(bk)
                            C.op("vector", lambda E, c=c, tt=tt, hh=hh, bk=bk:
                                 E.scalar_tensor_tensor(
                                     out=acc[:, (c * 4 + tt) * H + hh * 512:
                                             (c * 4 + tt) * H + (hh + 1) * 512],
                                     in0=PS[bk][:, :], scalar=scale_d[:, 0:1],
                                     in1=acc[:, (c * 4 + tt) * H + hh * 512:
                                             (c * 4 + tt) * H + (hh + 1) * 512],
                                     op0=ALU.mult, op1=ALU.add), inc="v")
                            release_bank(bk, "v")

        C.wait("sync", "v")
        for i in range(NT):
            C.op("sync", lambda E, i=i: E.dma_start(
                out=out_h[i * 128:(i + 1) * 128, :],
                in_=acc[:, i * H:(i + 1) * H]), inc="sp", by=16)
        C.wait("sync", "sp")

        with nc.Block() as block:
            @block.sync
            def _(E):
                for f in C.q["sync"]:
                    f(E)

            @block.vector
            def _(E):
                for f in C.q["vector"]:
                    f(E)

            @block.scalar
            def _(E):
                for f in C.q["scalar"]:
                    f(E)

            @block.tensor
            def _(E):
                for f in C.q["tensor"]:
                    f(E)

    nc._ctx_trace = C.trace
    return nc


def _fmaj(v):
    return np.ascontiguousarray(v.reshape(NF, 128).T)


def _bcast(s):
    return np.full((128, 1), s, dtype=np.float32)


def make_inputs(x, wp, wdl, bd, A1, B1, D, gw, uw, dw, sin, sind, T):
    common = {
        "wdp": np.ascontiguousarray(wp[0:H, :]),
        "wc": np.ascontiguousarray(wp[2 * H:3 * H, :]),
        "wres": np.ascontiguousarray(wp[3 * H:4 * H, :]),
        "wdl": wdl, "bd": _fmaj(bd), "a1": _fmaj(A1), "b1": _fmaj(B1),
        "dvec": _bcast(D), "sin": _bcast(sin), "sind": _bcast(sind),
        "gw": gw, "uw": uw, "dw": dw,
        "ident": np.eye(128, dtype=np.float32),
        "ones1": np.ones((1, 128), dtype=np.float32),
    }
    B = x.shape[0]
    in_maps = []
    for core in range(2 * B):
        b, half = core // 2, core % 2
        m = dict(common)
        m["x_own"] = np.ascontiguousarray(x[b, half * T:(half + 1) * T, :])
        m["x_oth"] = np.ascontiguousarray(x[b, (1 - half) * T:(2 - half) * T, :])
        m["gate"] = _bcast(1.0 if half == 1 else 0.0)
        in_maps.append(m)
    return in_maps


def kernel(L0_attn_x, L0_attn_W_proj, L0_attn_A, L0_attn_Bp, L0_attn_D,
           L0_attn_W_delta, L0_attn_b_delta, L0_ffn_gate_w, L0_ffn_up_w,
           L0_ffn_down_w, L0_ffn_s_in, L0_ffn_s_in_down):
    B, L, _ = L0_attn_x.shape
    T = L // 2
    x = np.ascontiguousarray(np.asarray(L0_attn_x, dtype=np.float32))
    wp = np.ascontiguousarray(np.asarray(L0_attn_W_proj, dtype=np.float32))
    A1 = np.ascontiguousarray(np.asarray(L0_attn_A, dtype=np.float32)[0, 0])
    B1 = np.ascontiguousarray(np.asarray(L0_attn_Bp, dtype=np.float32)[0, :, 0])
    D = np.asarray(L0_attn_D, dtype=np.float32).reshape(-1)[0]
    wdl = np.ascontiguousarray(np.asarray(L0_attn_W_delta, dtype=np.float32))
    bd = np.ascontiguousarray(np.asarray(L0_attn_b_delta, dtype=np.float32))
    gw = np.ascontiguousarray(np.asarray(L0_ffn_gate_w, dtype=np.float32))
    uw = np.ascontiguousarray(np.asarray(L0_ffn_up_w, dtype=np.float32))
    dw = np.ascontiguousarray(np.asarray(L0_ffn_down_w, dtype=np.float32))
    sin = np.asarray(L0_ffn_s_in, dtype=np.float32).reshape(-1)[0]
    sind = np.asarray(L0_ffn_s_in_down, dtype=np.float32).reshape(-1)[0]

    in_maps = make_inputs(x, wp, wdl, bd, A1, B1, D, gw, uw, dw, sin, sind, T)
    nc = build_program(T, T, dbg=False)
    res = run_bass_kernel_spmd(nc, in_maps, core_ids=list(range(8)))
    out = np.empty((B, L, H), dtype=np.float32)
    for core in range(2 * B):
        b, half = core // 2, core % 2
        out[b, half * T:(half + 1) * T, :] = res.results[core]["out"]
    return out
